# version 19
# baseline (speedup 1.0000x reference)
import numpy as np
from contextlib import ExitStack

import concourse.bass as bass
import concourse.mybir as mybir
from concourse.bass_utils import run_bass_kernel_spmd
from concourse.replica_groups import maybe_share_collective_output_space

F32 = mybir.dt.float32
F32R = mybir.dt.float32r
BF16 = mybir.dt.bfloat16
U8 = mybir.dt.uint8
AF = mybir.ActivationFunctionType
ALU = mybir.AluOpType
AXX = mybir.AxisListType.X

NC = 8
T = 8192
TC = T // NC
D = 2048
H = 16
DH = 128
BLK = 256
NBLK = TC // BLK
KCN = D // 128
NTC = TC // 128
E = 8
IFF = 1024
ICN = IFF // 128
MODFC = 6 * D // NC // 128
MODF = MODFC * 128
DKC = 784
LNEPS = 1e-5
RMSEPS = 1e-6


class KB:
    ENGS = ("sync", "tensor", "vector", "scalar", "gpsimd")
    DMA_SEM = {"sync": "sD", "scalar": "sA", "gpsimd": "sG"}
    CMP_SEM = {"tensor": "sP", "vector": "sV", "scalar": "sS", "gpsimd": "sGc"}

    def __init__(self):
        self.q = {e: [] for e in self.ENGS}
        self.cnt = {}
        for s in list(self.DMA_SEM.values()) + list(self.CMP_SEM.values()) + ["sCC"]:
            self.cnt[s] = 0
        self.seen = {e: {} for e in self.ENGS}

    def op(self, eng, fn, deps=(), sig=None, inc=1):
        waits = []
        for dtok in deps:
            if dtok is None:
                continue
            s, v = dtok
            if v > self.seen[eng].get(s, 0):
                self.seen[eng][s] = v
                waits.append((s, v))
        tok = None
        if sig is not None:
            if sig not in self.cnt:
                self.cnt[sig] = 0
            self.cnt[sig] += inc
            tok = (sig, self.cnt[sig])
        self.q[eng].append((waits, fn, sig, inc))
        return tok

    def dma(self, eng, out, in_, deps=(), sem=None, chain=True, **kw):
        sem = sem or self.DMA_SEM[eng]
        if sem not in self.cnt:
            self.cnt[sem] = 0
        deps = list(deps)
        if chain and self.cnt[sem] > 0:
            deps.append((sem, self.cnt[sem]))
        return self.op(eng, lambda e: e.dma_start(out=out, in_=in_, **kw),
                       deps, sem, 16)

    def comp(self, eng, fn, deps=(), sig=False):
        return self.op(eng, fn, deps, self.CMP_SEM[eng] if sig else None, 1)

    def fence(self, sem):
        return (sem, self.cnt.get(sem, 0))


def _prep_weights(w_qkv, w_gate, w_out, wmod_, gate_w, w1, w2, w3, ws1, ws2, ws3):
    out = {}
    w = np.asarray(w_qkv, dtype=np.float32).reshape(3, H, DH, KCN, 128)
    out["wqkv"] = np.ascontiguousarray(w.transpose(1, 0, 4, 3, 2)).reshape(H, 3, 128, D)
    g = np.asarray(w_gate, dtype=np.float32).reshape(KCN, 128, KCN, 128)
    out["wgate"] = np.ascontiguousarray(g.transpose(0, 3, 2, 1)).reshape(KCN, 128, D)
    wo = np.asarray(w_out, dtype=np.float32).T.reshape(KCN, 128, 2, 1024)
    out["wout"] = np.ascontiguousarray(wo.transpose(2, 0, 1, 3))
    wm = np.asarray(wmod_, dtype=np.float32).reshape(KCN, 128, 96, 128)
    out["wmod_percore"] = [
        np.ascontiguousarray(wm[:, :, c * MODFC:(c + 1) * MODFC, :]
                             .transpose(2, 1, 0, 3)).reshape(MODFC, 128, D)
        for c in range(NC)]
    gw = np.asarray(gate_w, dtype=np.float32).reshape(E, KCN, 128)
    gwp = np.zeros((128, KCN, 128), dtype=np.float32)
    gwp[:, :, 0:E] = gw.transpose(2, 1, 0)
    out["gatew"] = np.ascontiguousarray(gwp).reshape(128, KCN * 128)
    w1r = np.asarray(w1, dtype=np.float32).reshape(E, ICN, 128, KCN, 128)
    w3r = np.asarray(w3, dtype=np.float32).reshape(E, ICN, 128, KCN, 128)
    out["w13"] = np.ascontiguousarray(
        np.stack([w1r.transpose(0, 1, 4, 3, 2), w3r.transpose(0, 1, 4, 3, 2)],
                 axis=2)).reshape(E, ICN, 2, 128, D)
    s1 = np.asarray(ws1, dtype=np.float32).reshape(ICN, 128, KCN, 128)
    s3 = np.asarray(ws3, dtype=np.float32).reshape(ICN, 128, KCN, 128)
    out["ws13"] = np.ascontiguousarray(
        np.stack([s1.transpose(0, 3, 2, 1), s3.transpose(0, 3, 2, 1)], axis=1)).reshape(ICN, 2, 128, D)
    w2t = np.asarray(w2, dtype=np.float32).transpose(0, 2, 1).reshape(
        E, ICN, 128, 4, 512)
    out["w2"] = np.ascontiguousarray(w2t.transpose(0, 3, 1, 2, 4))
    ws2t = np.asarray(ws2, dtype=np.float32).T.reshape(ICN, 128, 4, 512)
    out["ws2"] = np.ascontiguousarray(ws2t.transpose(2, 0, 1, 3))
    return out


def _dk_const(core):
    row = np.zeros((DKC,), dtype=np.float32)
    row[0:256] = np.arange(1, 257)
    row[256:512] = 255.0 - np.arange(256)
    row[512:768] = -np.arange(1, 257)
    row[768:771] = (256.0, 512.0, 768.0)
    for j in range(NC):
        row[771 + j] = 1024.0 * (core - 1 - j) if j < core else 3.0e5
    return np.ascontiguousarray(np.broadcast_to(row, (H, DKC)))


def _mask01():
    m = np.zeros((128, 512), dtype=np.float32)
    for c in range(2):
        jj = c * 128 + np.arange(128)[:, None]
        ii = np.arange(256)[None, :]
        m[:, c * 256:(c + 1) * 256] = (ii >= jj).astype(np.float32)
    return m


def build_nc(debug=False):
    nc = bass.Bass("TRN2", target_bir_lowering=False)
    kb = KB()

    x_in = nc.dram_tensor("x_in", [TC, D], F32, kind="ExternalInput").ap()
    wqkv = nc.dram_tensor("wqkv", [H, 3, 128, D], F32, kind="ExternalInput").ap()
    wgate = nc.dram_tensor("wgate", [KCN, 128, D], F32, kind="ExternalInput").ap()
    wout = nc.dram_tensor("wout", [2, KCN, 128, 1024], F32, kind="ExternalInput").ap()
    wmod = nc.dram_tensor("wmod", [MODFC, 128, D], F32, kind="ExternalInput").ap()
    gatew = nc.dram_tensor("gatew", [128, KCN * 128], F32, kind="ExternalInput").ap()
    w13 = nc.dram_tensor("w13", [E, ICN, 2, 128, D], F32, kind="ExternalInput").ap()
    ws13 = nc.dram_tensor("ws13", [ICN, 2, 128, D], F32, kind="ExternalInput").ap()
    w2d = nc.dram_tensor("w2d", [E, 4, ICN, 128, 512], F32, kind="ExternalInput").ap()
    ws2d = nc.dram_tensor("ws2d", [4, ICN, 128, 512], F32, kind="ExternalInput").ap()
    slope = nc.dram_tensor("slope", [H, 1], F32, kind="ExternalInput").ap()
    cvec = nc.dram_tensor("cvec", [D], F32, kind="ExternalInput").ap()
    normw = nc.dram_tensor("normw", [D], F32, kind="ExternalInput").ap()
    dkconst = nc.dram_tensor("dkconst", [H, DKC], F32, kind="ExternalInput").ap()
    mask01d = nc.dram_tensor("mask01d", [128, 512], F32, kind="ExternalInput").ap()
    identd = nc.dram_tensor("identd", [128, 128], F32, kind="ExternalInput").ap()
    y_out = nc.dram_tensor("y_out", [TC, D], F32, kind="ExternalOutput").ap()

    rg = [list(range(NC))]
    sp = maybe_share_collective_output_space("AllGather", rg, prefer_local=True)
    agm_in = nc.dram_tensor("agm_in", [MODF], F32).ap()
    agm_out = nc.dram_tensor("agm_out", [NC, MODF], F32, addr_space=sp).ap()
    agk_in = nc.dram_tensor("agk_in", [H, 128, 128], F32).ap()
    agk_out = nc.dram_tensor("agk_out", [NC, H, 128, 128], F32, addr_space=sp).ap()
    dk_dram = nc.dram_tensor("dk_dram", [H, DKC], F32).ap()
    dk16_dram = nc.dram_tensor("dk16_dram", [H * 16], F32).ap()
    rows_dram = nc.dram_tensor("rows_dram", [4, D], F32).ap()
    rstd_dram = nc.dram_tensor("rstd_dram", [TC], BF16).ap()

    TOTAL = 212448
    backing = nc.alloc_sbuf_tensor("arena", [128, TOTAL], U8)
    base = nc.lookup_mloc(backing).addr
    made = {}

    def sb(name, shape, dtype, off):
        assert name not in made
        nbytes = int(np.prod(shape[1:])) * mybir.dt.size(dtype)
        assert off + nbytes <= TOTAL, f"{name}: {off}+{nbytes} > {TOTAL}"
        h = nc.virtual_sbuf_tensor(name, list(shape), dtype)
        nc._tensor(h.name, list(shape), dtype, type="SB")
        mloc = nc.lookup_mloc(h)
        mloc.allocated = True
        mloc.addr = base + off
        made[name] = (off, nbytes)
        return h.ap()

    ident32 = sb("ident32", [128, 128], F32, 0)
    identb = sb("identb", [128, 128], BF16, 512)
    O_A1 = 800
    h1T = sb("h1T", [128, KCN * TC], BF16, O_A1)
    x1 = sb("x1", [128, NTC * D], F32, O_A1)
    O_A25 = O_A1 + 65536
    xtok = sb("xtok", [128, NTC * D], F32, O_A25)
    attn = sb("attn", [128, H * TC], BF16, O_A25)
    epi = sb("epi", [128, KCN * TC], BF16, O_A25 + 32768)
    h2T = sb("h2T", [128, KCN * TC], BF16, O_A25)
    O_HI = O_A25 + 32768
    qkvh = [sb(f"qkvh{i}", [128, 3 * TC], BF16, O_HI + i * 6144) for i in range(2)]
    wqkvb = [sb(f"wqkvb{i}", [128, 1024], BF16, O_A1 + 32768 + i * 2048)
             for i in range(4)]
    kvf = sb("kvf", [128, H * 128], F32, O_HI + 20480)
    kvbf = sb("kvbf", [128, H * 128], BF16, O_HI + 28672)
    gatebuf = [sb(f"gatebuf{i}", [128, D], BF16, O_HI + i * 4096) for i in range(2)]
    kvall = sb("kvall", [128, H * 128], F32, O_HI + 8192)
    O_A3 = O_A25 + 65536
    qd = sb("qd", [128, H * TC], BF16, O_A3)
    B_1ca = sb("B_1ca", [128, D], F32, O_A3)
    B_sa = sb("B_sa", [128, D], F32, O_A3 + 8192)
    rowA = sb("rowA", [1, D], F32, O_A3 + 16384)
    rowB = sb("rowB", [1, D], F32, O_A3 + 24576)
    woutb = sb("woutb", [128, KCN * 1024], BF16, O_A3)
    B_rstd = sb("B_rstd", [128, TC], BF16, O_A3 + 32768 + 32768 + 8192 + 3136)
    B_1cb = sb("B_1cb", [128, D], F32, O_A3)
    B_sb = sb("B_sb", [128, D], F32, O_A3 + 8192)
    B_invgb = sb("B_invgb", [128, D], F32, O_A3 + 16384)
    gff = sb("gff", [128, ICN * TC], BF16, O_A3)
    B_gb = sb("B_gb", [128, D], F32, O_A3 + 32768)
    O_A4 = O_A3 + 32768
    gate = sb("gate", [128, KCN * TC], BF16, O_A4)
    wmodb = [sb(f"wmodb{i}", [128, D], F32, O_A4 + i * 8192) for i in range(2)]
    woutstage = [sb(f"woutstage{i}", [128, 1024], F32, O_A4 + i * 4096)
                 for i in range(2)]
    B_ga = sb("B_ga", [128, D], F32, O_A4 + 16384)
    scorest = sb("scorest", [8, TC], F32, O_A4)
    w13b = [sb(f"w13b{i}", [128, D], BF16, O_A3 + 16384 + i * 4096) for i in range(2)]
    w2b = [sb(f"w2b{i}", [128, 2048], BF16, O_A3 + 24576 + i * 4096)
           for i in range(2)]
    O_SCR = O_A4 + 32768
    t1 = sb("t1", [128, D], F32, O_SCR)
    sqs = sb("sqs", [128, TC], BF16, O_SCR)
    t1b = sb("t1b", [128, TC], BF16, O_SCR + 2048)
    rstd_rowf = sb("rstd_rowf", [1, TC], F32, O_SCR + 4096)
    rstd_rowb = sb("rstd_rowb", [1, TC], BF16, O_SCR + 2048)
    mask01 = sb("mask01", [128, 512], F32, O_SCR)
    dmask = [sb(f"dmask{i}", [128, 512], F32, O_SCR + 2048 + i * 2048)
             for i in range(2)]
    aT = [sb(f"aT{i}", [128, 256], BF16, O_SCR + 6144 + i * 512) for i in range(2)]
    qdb = sb("qdb", [128, 256], BF16, O_SCR + 7168)
    p2b = sb("p2b", [128, 256], BF16, O_SCR + 7680)
    silus = sb("silus", [128, 512], F32, O_SCR + 2048)
    O_Z = O_SCR + 8192
    dk_sb = sb("dk_sb", [H, DKC], F32, O_Z)
    qdectall = sb("qdectall", [128, H * 256], BF16, O_A4 + 16384)
    percol = sb("percol", [128, 256], F32, O_Z + 4160)
    O_CP = O_Z + 5184
    cols = sb("cols", [128, 64], F32, O_CP)
    c_fm = sb("c_fm", [128, 16], F32, O_CP + 256)
    c_silu = sb("c_silu", [128, 16], F32, O_CP + 320)
    modcol = sb("modcol", [128, MODFC], F32, O_CP + 384)
    epscol = sb("epscol", [128, 1], F32, O_CP + 448)
    normw_col = sb("normw_col", [128, 16], F32, O_CP + 464)
    ones_bf = sb("ones_bf", [128, 128], BF16, O_CP + 528)
    kdcol = sb("kdcol", [128, 2], F32, O_CP + 800)
    comb = sb("comb", [128, 64], F32, O_CP + 816)
    smt = sb("smt", [128, 40], F32, O_CP + 1088)
    sctok = sb("sctok", [128, 8], F32, O_CP + 1248)
    gatew_sb = sb("gatew_sb", [128, KCN * 128], BF16, O_A4 + 4096)
    s16 = sb("s16", [H, 1], F32, O_CP + 1312)
    negs16 = sb("negs16", [H, 1], F32, O_CP + 1328)
    eps6 = sb("eps6", [128, 1], F32, O_CP + 1344)
    assert O_CP + 1600 <= TOTAL

    ps = [nc.alloc_psum_tensor(f"ps{i}", [128, 512], F32).ap() for i in range(6)]
    pb = [nc.alloc_psum_tensor(f"pb{i}", [128, 1024], BF16).ap() for i in range(2)]

    class BankSet:
        def __init__(self, aps):
            self.aps = aps
            self.avail = [(i, None) for i in range(len(aps))]

        def get(self):
            j, tok = self.avail.pop(0)
            return j, self.aps[j], tok

        def release(self, j, tok):
            self.avail.append((j, tok))

    fb = BankSet(ps)
    bb = BankSet(pb)

    def row1(ap_flat, a, b):
        return ap_flat[a:b].rearrange("(o n) -> o n", o=1)

    def bcast(ap_flat, a, b):
        return row1(ap_flat, a, b).partition_broadcast(128)[:, 0, :]

    def colap(ap_flat, a, b):
        return ap_flat[a:b].rearrange("(p o) -> p o", o=1)

    dbg_outs = {}

    def dump(name, src_ap, shape):
        if not debug:
            return
        d = nc.dram_tensor("dbg_" + name, list(shape), src_ap.dtype,
                           kind="ExternalOutput").ap()
        t = kb.dma("gpsimd", d, src_ap,
                   deps=[kb.fence("sV"), kb.fence("sS"), kb.fence("sP"),
                         kb.fence("sA"), kb.fence("sG")])
        for eng in ("vector", "scalar", "tensor", "gpsimd", "sync"):
            kb.op(eng, lambda e: e.nop(), deps=[t])
        dbg_outs[name] = shape

    spacecol = sb("spacecol", [128, 2], F32, O_CP + 1360)

    def vsp():
        pass

    t_ident = kb.dma("gpsimd", ident32, identd)
    t_identb = kb.comp("vector", lambda e: e.tensor_copy(out=identb, in_=ident32),
                       deps=[t_ident], sig=True)
    t_dkc = kb.dma("gpsimd", dk_sb, dkconst)
    t_slope = kb.dma("gpsimd", s16, slope)
    t_cv = kb.dma("gpsimd", c_fm, cvec.rearrange("(k p) -> p k", p=128),
                  allow_slow_non_contiguous=True)
    t_nw = kb.dma("gpsimd", normw_col, normw.rearrange("(k p) -> p k", p=128),
                  allow_slow_non_contiguous=True)

    kb.comp("vector", lambda e: e.memset(spacecol, 0.0))
    kb.comp("vector", lambda e: e.memset(epscol, LNEPS))
    kb.comp("vector", lambda e: e.memset(eps6, RMSEPS))
    t_ones = kb.comp("vector", lambda e: e.memset(ones_bf, 1.0), sig=True)
    t_negs = kb.comp("vector", lambda e: e.tensor_scalar_mul(
        out=negs16, in0=s16, scalar1=-1.0), deps=[t_slope], sig=True)
    t_dk = kb.comp("scalar", lambda e: e.activation(
        out=dk_sb, in_=dk_sb, func=AF.Exp, bias=0.0, scale=negs16),
        deps=[t_dkc, t_negs], sig=True)
    t_dkd = kb.dma("gpsimd", dk_dram, dk_sb, deps=[t_dk])
    t_dk16 = kb.dma("gpsimd", dk16_dram.rearrange("(a b) -> a b", b=16),
                    dk_sb[:, 768:784], deps=[t_dk])
    t_percol = kb.dma("gpsimd", percol, bcast(dk16_dram, 0, 256),
                      deps=[t_dk16])
    t_csilu = kb.comp("scalar", lambda e: e.activation(
        out=c_silu, in_=c_fm, func=AF.Silu, bias=0.0, scale=1.0),
        deps=[t_cv], sig=True)

    t_x, t_stat = [], []
    for tc in range(NTC):
        tx = kb.dma("scalar", xtok[:, tc * D:(tc + 1) * D],
                    x_in[tc * 128:(tc + 1) * 128, :])
        t_x.append(tx)
        tid = kb.comp("scalar", lambda e, tc=tc: e.activation(
            out=xtok[:, tc * D:(tc + 1) * D],
            in_=xtok[:, tc * D:(tc + 1) * D], func=AF.Identity,
            bias=0.0, scale=1.0, accum_out=cols[:, tc:tc + 1]), deps=[tx],
            sig=True)
        t_stat.append(kb.comp("scalar", lambda e, tc=tc: e.activation(
            out=t1, in_=xtok[:, tc * D:(tc + 1) * D], func=AF.Square,
            bias=0.0, scale=1.0, accum_out=cols[:, 8 + tc:9 + tc]),
            deps=[tid], sig=True))
    t_m8 = kb.comp("vector", lambda e: e.tensor_scalar_mul(
        out=cols[:, 16:24], in0=cols[:, 0:8], scalar1=1.0 / D),
        deps=[t_stat[-1]], sig=True)
    t_msq = kb.comp("scalar", lambda e: e.activation(
        out=cols[:, 24:32], in_=cols[:, 16:24], func=AF.Square, bias=0.0,
        scale=1.0), deps=[t_m8], sig=True)
    t_var = kb.comp("vector", lambda e: e.scalar_tensor_tensor(
        out=cols[:, 32:40], in0=cols[:, 8:16], scalar=1.0 / D,
        in1=cols[:, 24:32], op0=ALU.mult, op1=ALU.subtract),
        deps=[t_msq], sig=True)
    t_sd = kb.comp("scalar", lambda e: e.activation(
        out=cols[:, 48:56], in_=cols[:, 32:40], func=AF.Sqrt, bias=epscol,
        scale=1.0), deps=[t_var], sig=True)
    t_r8 = kb.comp("vector", lambda e: e.reciprocal(
        out=cols[:, 40:48], in_=cols[:, 48:56]), deps=[t_sd], sig=True)

    wm_free = [None, None]
    for fc in range(MODFC):
        slot = fc % 2
        twl = kb.dma("sync", wmodb[slot], wmod[fc], deps=[wm_free[slot]],
                     sem=f"m{slot}")
        j, bank, bfree = fb.get()
        last = None
        for kc in range(KCN):
            last = kb.comp("tensor", lambda e, slot=slot, kc=kc, bank=bank:
                           e.matmul(bank[:, 0:1],
                                    wmodb[slot][:, kc * 128:(kc + 1) * 128],
                                    c_silu[:, kc:kc + 1],
                                    start=(kc == 0), stop=(kc == KCN - 1)),
                           deps=[twl, t_csilu, bfree],
                           sig=(kc == KCN - 1))
        wm_free[slot] = last
        tev = kb.comp("vector", lambda e, fc=fc, bank=bank: e.tensor_copy(
            out=modcol[:, fc:fc + 1], in_=bank[:, 0:1]), deps=[last], sig=True)
        fb.release(j, tev)
        kb.dma("gpsimd", colap(agm_in, fc * 128, (fc + 1) * 128),
               modcol[:, fc:fc + 1], deps=[tev])
    t_agm = kb.op("gpsimd", lambda e: e.collective_compute(
        "AllGather", ALU.bypass, replica_groups=rg, ins=[agm_in.opt()],
        outs=[agm_out.opt()]), deps=[kb.fence("sG")], sig="sCC", inc=1)

    agm_flat = agm_out.rearrange("a b -> (a b)")
    t_b1ca = kb.dma("gpsimd", B_1ca, bcast(agm_flat, 1 * D, 2 * D), deps=[t_agm])
    t_b1ca = kb.comp("vector", lambda e: e.tensor_scalar_add(
        out=B_1ca, in0=B_1ca, scalar1=1.0), deps=[t_b1ca], sig=True)
    t_bsa = kb.dma("gpsimd", B_sa, bcast(agm_flat, 0, D), deps=[t_agm])

    def clamp_row(src_k, dst_i, deps):
        tload = kb.dma("gpsimd", rowA, row1(agm_flat, src_k * D, (src_k + 1) * D),
                       deps=deps)
        tabs = kb.comp("scalar", lambda e: e.activation(
            out=rowB, in_=rowA, func=AF.Abs, bias=0.0, scale=1.0),
            deps=[tload], sig=True)
        tmax = kb.comp("vector", lambda e: e.tensor_scalar_max(
            out=rowB, in0=rowB, scalar1=0.25), deps=[tabs], sig=True)
        tge = kb.comp("vector", lambda e: e.tensor_scalar(
            out=rowA, in0=rowA, scalar1=0.0, scalar2=None, op0=ALU.is_ge),
            deps=[tload, tabs], sig=True)
        tsgn = kb.comp("vector", lambda e: e.tensor_scalar(
            out=rowA, in0=rowA, scalar1=2.0, scalar2=-1.0, op0=ALU.mult,
            op1=ALU.add), deps=[tge], sig=True)
        tmul = kb.comp("vector", lambda e: e.tensor_mul(
            out=rowA, in0=rowA, in1=rowB), deps=[tsgn, tmax], sig=True)
        tst = kb.dma("gpsimd", rows_dram[dst_i].rearrange("(o n) -> o n", o=1),
                     rowA, deps=[tmul])
        return tst, tmul

    t_garow, _ = clamp_row(2, 0, [t_agm])
    t_gbrow, t_gbmul = clamp_row(5, 1, [t_garow])
    t_invr = kb.comp("vector", lambda e: e.reciprocal(out=rowB, in_=rowA),
                     deps=[t_gbmul], sig=True)
    t_invrow = kb.dma("gpsimd", rows_dram[2].rearrange("(o n) -> o n", o=1),
                      rowB, deps=[t_invr])

    prev_tr = None
    t_h1 = None
    for tc in range(NTC):
        tts = kb.comp("vector", lambda e, tc=tc: e.tensor_scalar(
            out=t1, in0=xtok[:, tc * D:(tc + 1) * D],
            scalar1=cols[:, 16 + tc:17 + tc], scalar2=cols[:, 40 + tc:41 + tc],
            op0=ALU.subtract, op1=ALU.mult),
            deps=[t_r8, t_x[tc], prev_tr], sig=True)
        tt1 = kb.comp("vector", lambda e: e.tensor_mul(
            out=t1, in0=t1, in1=B_1ca), deps=[tts, t_b1ca], sig=True)
        tt2 = kb.comp("vector", lambda e: e.tensor_add(
            out=t1, in0=t1, in1=B_sa), deps=[tt1, t_bsa], sig=True)
        for kc in range(KCN):
            j, bank, bfree = fb.get()
            ttr = kb.comp("tensor", lambda e, kc=kc, bank=bank: e.transpose(
                bank[:, 0:128], t1[:, kc * 128:(kc + 1) * 128], ident32),
                deps=[tt2, bfree, t_ident], sig=True)
            t_h1 = kb.comp("scalar", lambda e, tc=tc, kc=kc, bank=bank: e.copy(
                out=h1T[:, kc * TC + tc * 128:kc * TC + (tc + 1) * 128],
                in_=bank[:, 0:128]), deps=[ttr], sig=True)
            fb.release(j, t_h1)
            prev_tr = ttr

    dump("h1T", h1T, [128, KCN * TC])
    dump("b1ca", B_1ca, [128, D])
    dump("bsa", B_sa, [128, D])
    dump("cols0", cols, [128, 64])

    t_mask = kb.dma("gpsimd", mask01, mask01d, deps=[t_h1, prev_tr])

    kvz1 = kb.comp("vector", lambda e: e.memset(kvf, 0.0),
                   deps=[t_h1], sig=True)
    kvz2 = kb.comp("vector", lambda e: e.memset(kvbf, 0.0),
                   deps=[t_h1], sig=True)

    for h in range(H):
        kb.dma("gpsimd", qdectall[:, h * 256:(h + 1) * 256],
               bcast(dk_dram[h], 0, 256), deps=[t_dkd], sem="qda", chain=False)
    t_qda = kb.fence("qda")

    wq_free = [None, None, None, None]
    slot_free = [None, None]
    dmask_free = [None, None]
    qdec_free = [None, None]
    t_kv_dma = []
    t_qd = {}
    t_p1 = {}
    t_attn_w = {}
    prev_tdm = [None, None]
    prev_outgrp = [None, None]

    for h in range(H):
        slot = h % 2
        qs = qkvh[slot]
        dsem = f"dh{h % 2}"
        tkd0 = kb.dma("sync", kdcol[:, 0:1], colap(dk_dram[h], 256, 384),
                      deps=[t_dkd, t_p1.get(h - 1)], sem=dsem, chain=False)
        tkd1 = kb.dma("sync", kdcol[:, 1:2], colap(dk_dram[h], 384, 512),
                      deps=[t_dkd], sem=dsem, chain=False)
        dslot = dmask[h % 2]
        tdm0 = kb.dma("sync", dslot[:, 0:256], bcast(dk_dram[h], 0, 256),
                      deps=[t_dkd, dmask_free[h % 2], t_mask], sem=dsem,
                      chain=False)
        tdm1 = kb.dma("sync", dslot[:, 256:512], bcast(dk_dram[h], 0, 256),
                      deps=[t_dkd], sem=dsem, chain=False)
        texc = [kb.dma("sync", smt[:, 32 + c:33 + c],
                       colap(dk_dram[h], 512 + c * 128, 512 + (c + 1) * 128),
                       deps=[t_dkd, prev_tdm[c] if h > 0 else None], sem=dsem,
                       chain=False)
                for c in range(2)]
        t_dh = kb.fence(dsem)
        tkd0 = tkd1 = tdm0 = tdm1 = t_dh
        texc = [t_dh, t_dh]
        tqdec = t_qda
        tdm = []
        for c in range(2):
            tt = kb.comp("vector", lambda e, c=c, dslot=dslot: e.tensor_scalar(
                out=dslot[:, c * 256:(c + 1) * 256],
                in0=dslot[:, c * 256:(c + 1) * 256],
                scalar1=smt[:, 32 + c:33 + c], scalar2=None, op0=ALU.mult),
                deps=[tdm0, tdm1, texc[c]], sig=True)
            tt = kb.comp("vector", lambda e, c=c, dslot=dslot: e.tensor_mul(
                out=dslot[:, c * 256:(c + 1) * 256],
                in0=dslot[:, c * 256:(c + 1) * 256],
                in1=mask01[:, c * 256:(c + 1) * 256]),
                deps=[tt, t_mask], sig=True)
            tdm.append(tt)
        prev_tdm = tdm

        part_done = []
        for p in range(3):
            banksets = [fb.get() for _ in range(2)]
            for wh in range(2):
                wsl = (h % 2) * 2 + wh
                ws = wqkvb[wsl]
                twl = kb.dma("gpsimd", ws, wqkv[h, p, :, wh * 1024:(wh + 1) * 1024],
                             deps=[wq_free[wsl]], sem=f"q{wsl}")
                for th in range(2):
                    jj, bank, bfree = banksets[th]
                    for k8 in range(8):
                        kc = wh * 8 + k8
                        tok = kb.comp(
                            "tensor",
                            lambda e, ws=ws, k8=k8, kc=kc, th=th, bank=bank:
                            e.matmul(
                                bank,
                                ws[:, k8 * 128:(k8 + 1) * 128],
                                h1T[:, kc * TC + th * 512:kc * TC + th * 512 + 512],
                                start=(kc == 0), stop=(kc == KCN - 1)),
                            deps=[twl, bfree],
                            sig=(k8 == 7))
                        if k8 == 7:
                            wq_free[wsl] = tok
                            if wh == 1:
                                last_mm = tok
            for th in range(2):
                jj, bank, bfree = banksets[th]
                tev = kb.comp("scalar", lambda e, p=p, th=th, bank=bank, qs=qs:
                              e.activation(
                                  out=qs[:, p * TC + th * 512:p * TC + th * 512 + 512],
                                  in_=bank, func=AF.Silu, bias=0.0, scale=1.0),
                              deps=[last_mm] + (slot_free[slot] or []), sig=True)
                part_done.append(tev)
                fb.release(jj, tev)
        t_q, t_k, t_v = part_done[1], part_done[3], part_done[5]

        tqd = None
        for b in range(NBLK):
            tqd = kb.comp("vector", lambda e, h=h, b=b, qs=qs:
                          e.tensor_mul(
                              out=qd[:, h * TC + b * 256:h * TC + b * 256 + 256],
                              in0=qs[:, 0 * TC + b * 256:0 * TC + b * 256 + 256],
                              in1=qdectall[:, h * 256:(h + 1) * 256]),
                          deps=[t_q, tqdec], sig=True)
        t_qd[h] = tqd

        tv_tok = []
        for tcc in range(NTC):
            j, bank, bfree = bb.get()
            ttr = kb.comp("tensor", lambda e, tcc=tcc, qs=qs, bank=bank:
                          e.transpose(bank[:, 0:128],
                                      qs[:, 2 * TC + tcc * 128:2 * TC + (tcc + 1) * 128],
                                      identb),
                          deps=[t_v, bfree, t_identb], sig=True)
            tev = kb.comp("scalar", lambda e, tcc=tcc, qs=qs, bank=bank:
                          e.copy(out=qs[:, 2 * TC + tcc * 128:2 * TC + (tcc + 1) * 128],
                                 in_=bank[:, 0:128]),
                          deps=[ttr], sig=True)
            bb.release(j, tev)
            tv_tok.append(tev)

        if h == 0:
            dump("qs0", qs, [128, 3 * TC])
            dump("dmask0", dmask[0], [128, 512])
            dump("qdect0", qdectall[:, 0:256], [128, 256])
        t_scan_last = None
        attn_writes = []
        for b in range(NBLK):
            kqev = []
            for kk in range(2):
                j, bank, bfree = fb.get()
                tmm = kb.comp("tensor", lambda e, b=b, kk=kk, qs=qs, bank=bank:
                              e.matmul(
                                  bank[:, 0:256],
                                  qs[:, 1 * TC + b * 256 + kk * 128:
                                      1 * TC + b * 256 + kk * 128 + 128],
                                  qs[:, 0 * TC + b * 256:0 * TC + b * 256 + 256],
                                  start=True, stop=True),
                              deps=[t_q, t_k, bfree], sig=True)
                tat = kb.comp("vector", lambda e, kk=kk, bank=bank, dslot=dslot:
                              e.tensor_mul(out=aT[kk], in0=bank[:, 0:256],
                                           in1=dslot[:, kk * 256:(kk + 1) * 256]),
                              deps=[tmm, tdm[kk], prev_outgrp[kk]], sig=True)
                fb.release(j, tat)
                kqev.append(tat)
            tk_tok = []
            for kk in range(2):
                tcc = b * 2 + kk
                j, bank, bfree = bb.get()
                ttr = kb.comp("tensor", lambda e, tcc=tcc, qs=qs, bank=bank:
                              e.transpose(bank[:, 0:128],
                                          qs[:, 1 * TC + tcc * 128:
                                              1 * TC + (tcc + 1) * 128],
                                          identb),
                              deps=[kqev[1], bfree, t_identb], sig=True)
                tev = kb.comp("vector", lambda e, tcc=tcc, kk=kk, qs=qs, bank=bank:
                              e.tensor_scalar(
                                  out=qs[:, 1 * TC + tcc * 128:1 * TC + (tcc + 1) * 128],
                                  in0=bank[:, 0:128], scalar1=kdcol[:, kk:kk + 1],
                                  scalar2=None, op0=ALU.mult),
                              deps=[ttr, tkd0, tkd1], sig=True)
                bb.release(j, tev)
                tk_tok.append(tev)
            j, bank, bfree = fb.get()
            for kk in range(2):
                kb.comp("tensor", lambda e, b=b, kk=kk, qs=qs, bank=bank:
                        e.matmul(bank[:, 0:256],
                                 qs[:, 2 * TC + (b * 2 + kk) * 128:
                                     2 * TC + (b * 2 + kk + 1) * 128],
                                 aT[kk], start=(kk == 0), stop=False),
                        deps=[kqev[kk], tv_tok[b * 2 + kk], bfree])
            tmm = kb.comp("tensor", lambda e, h=h, b=b, bank=bank:
                          e.matmul(bank[:, 0:256],
                                   kvbf[:, h * 128:(h + 1) * 128],
                                   qd[:, h * TC + b * 256:h * TC + b * 256 + 256],
                                   start=False, stop=True),
                          deps=[t_qd[h], kvz2 if b == 0 else t_scan_last],
                          sig=True)
            tev = kb.comp("scalar", lambda e, h=h, b=b, bank=bank:
                          e.copy(out=attn[:, h * TC + b * 256:h * TC + b * 256 + 256],
                                 in_=bank[:, 0:256]),
                          deps=[tmm], sig=True)
            fb.release(j, tev)
            attn_writes.append(tev)
            prev_outgrp = [tmm, tmm]
            if h == 0 and b == 0:
                dump("aT0", aT[0], [128, 256])
                dump("aT1", aT[1], [128, 256])
                dump("vtok0", qs, [128, 3 * TC])
            j2, bank2, bfree2 = fb.get()
            for kk in range(2):
                tok = kb.comp("tensor", lambda e, b=b, kk=kk, qs=qs, bank2=bank2:
                              e.matmul(bank2[:, 0:128],
                                       qs[:, 1 * TC + (b * 2 + kk) * 128:
                                           1 * TC + (b * 2 + kk + 1) * 128],
                                       qs[:, 2 * TC + (b * 2 + kk) * 128:
                                           2 * TC + (b * 2 + kk + 1) * 128],
                                       start=(kk == 0), stop=(kk == 1)),
                              deps=[tk_tok[kk], tv_tok[b * 2 + kk], bfree2],
                              sig=(kk == 1))
            tupd = kb.comp("vector", lambda e, h=h, bank2=bank2:
                           e.scalar_tensor_tensor(
                               out=kvf[:, h * 128:(h + 1) * 128],
                               in0=kvf[:, h * 128:(h + 1) * 128],
                               scalar=percol[:, h * 16:h * 16 + 1],
                               in1=bank2[:, 0:128], op0=ALU.mult, op1=ALU.add),
                           deps=[tok, kvz1, t_percol], sig=True)
            fb.release(j2, tupd)
            t_scan_last = kb.comp("scalar", lambda e, h=h:
                                  e.copy(out=kvbf[:, h * 128:(h + 1) * 128],
                                         in_=kvf[:, h * 128:(h + 1) * 128]),
                                  deps=[tupd], sig=True)
        t_kv_dma.append(kb.dma("gpsimd", agk_in[h], kvf[:, h * 128:(h + 1) * 128],
                               deps=[tupd]))
        dmask_free[h % 2] = kqev[1]
        slot_free[slot] = [tupd, t_qd[h], attn_writes[-1]]
        t_p1[h] = tupd
        t_attn_w[h] = attn_writes

    t_agk = kb.op("gpsimd", lambda e: e.collective_compute(
        "AllGather", ALU.bypass, replica_groups=rg, ins=[agk_in.opt()],
        outs=[agk_out.opt()]), deps=[kb.fence("sG"), t_agm], sig="sCC", inc=1)

    last_gate_reads = []
    t_combh = {h: None for h in range(H)}
    for jrank in range(NC):
        tkv = kb.dma("gpsimd", kvall.rearrange("p (h e) -> p h e", h=H),
                     agk_out[jrank].transpose([1, 0, 2]),
                     deps=[t_agk] + ((slot_free[0] or []) + (slot_free[1] or [])
                                     if jrank == 0 else [t_combh[H - 1]]))
        for h in range(H):
            if jrank == 0:
                tc0 = kb.comp("vector", lambda e, h=h, jrank=jrank: e.tensor_scalar(
                    out=kvf[:, h * 128:(h + 1) * 128],
                    in0=kvall[:, h * 128:(h + 1) * 128],
                    scalar1=percol[:, h * 16 + 3 + jrank:h * 16 + 4 + jrank],
                    scalar2=None, op0=ALU.mult),
                    deps=[tkv, t_percol, t_kv_dma[h], t_p1[H - 1]], sig=True)
            else:
                tc0 = kb.comp("vector", lambda e, h=h, jrank=jrank:
                              e.scalar_tensor_tensor(
                                  out=kvf[:, h * 128:(h + 1) * 128],
                                  in0=kvall[:, h * 128:(h + 1) * 128],
                                  scalar=percol[:, h * 16 + 3 + jrank:h * 16 + 4 + jrank],
                                  in1=kvf[:, h * 128:(h + 1) * 128],
                                  op0=ALU.mult, op1=ALU.add),
                              deps=[tkv, t_combh[h]], sig=True)
            t_combh[h] = tc0
        last_gate_reads = [t_combh[H - 1], t_combh[0], t_combh[7]]
    t_kvin = {}
    for h in range(H):
        t_kvin[h] = kb.comp("scalar", lambda e, h=h:
                            e.copy(out=kvbf[:, h * 128:(h + 1) * 128],
                                   in_=kvf[:, h * 128:(h + 1) * 128]),
                            deps=[t_combh[h]], sig=True)

    gb_free = [None, None]
    t_gate_mm = None
    t_gate_ev = {}
    for fc in range(KCN):
        slot = fc % 2
        twl = kb.dma("gpsimd", gatebuf[slot], wgate[fc],
                     deps=[gb_free[slot]] + (slot_free[0] or [])
                     + (slot_free[1] or []),
                     sem=f"gt{slot}")
        for th in range(2):
            j, bank, bfree = fb.get()
            for kc in range(KCN):
                t_gate_mm = kb.comp(
                    "tensor",
                    lambda e, slot=slot, kc=kc, th=th, bank=bank: e.matmul(
                        bank, gatebuf[slot][:, kc * 128:(kc + 1) * 128],
                        h1T[:, kc * TC + th * 512:kc * TC + th * 512 + 512],
                        start=(kc == 0), stop=(kc == KCN - 1)),
                    deps=[twl, bfree], sig=(kc == KCN - 1))
            if th == 1:
                gb_free[slot] = t_gate_mm
            tev = kb.comp("scalar", lambda e, fc=fc, th=th, bank=bank:
                          e.activation(
                              out=gate[:, fc * TC + th * 512:fc * TC + th * 512 + 512],
                              in_=bank, func=AF.Sigmoid, bias=0.0, scale=1.0),
                          deps=[t_gate_mm], sig=True)
            t_gate_ev[(fc, th)] = tev
            fb.release(j, tev)
    t_gate_done = kb.fence("sP")

    t_x1 = []
    for tc in range(NTC):
        t_x1.append(kb.dma("scalar", x1[:, tc * D:(tc + 1) * D],
                           x_in[tc * 128:(tc + 1) * 128, :],
                           deps=[t_gate_done]))

    dump("gate", gate, [128, KCN * TC])

    t_p2 = {}
    prev_qdb_read = None
    prev_p2b_read = None
    for h in range(H):
        last = None
        for b in range(NBLK):
            if b == 0:
                rhs_tok, rhs_ap = t_qd[h], qd[:, h * TC:h * TC + 256]
            else:
                rhs_tok = kb.comp("vector", lambda e, h=h, b=b: e.tensor_scalar(
                    out=qdb, in0=qd[:, h * TC + b * 256:h * TC + b * 256 + 256],
                    scalar1=percol[:, h * 16 + b - 1:h * 16 + b],
                    scalar2=None, op0=ALU.mult),
                    deps=[t_qd[h], prev_qdb_read], sig=True)
                rhs_ap = qdb
            j, bank, bfree = fb.get()
            tmm = kb.comp("tensor", lambda e, h=h, rhs_ap=rhs_ap, bank=bank:
                          e.matmul(bank[:, 0:256], kvbf[:, h * 128:(h + 1) * 128],
                                   rhs_ap, start=True, stop=True),
                          deps=[t_kvin[h], rhs_tok, bfree], sig=True)
            if b > 0:
                prev_qdb_read = tmm
            tcp = kb.comp("scalar", lambda e, bank=bank: e.copy(
                out=p2b, in_=bank[:, 0:256]),
                deps=[tmm, prev_p2b_read], sig=True)
            fb.release(j, tcp)
            last = kb.comp("vector", lambda e, h=h, b=b: e.tensor_add(
                out=attn[:, h * TC + b * 256:h * TC + b * 256 + 256],
                in0=attn[:, h * TC + b * 256:h * TC + b * 256 + 256],
                in1=p2b), deps=[tcp, t_attn_w[h][b]], sig=True)
            prev_p2b_read = last
        t_p2[h] = last
    t_pass2_pe = kb.fence("sP")

    dump("attn", attn, [128, H * TC])
    dump("kvin", kvf, [128, H * 128])

    stat_banks = [fb.get(), fb.get()]
    t_sq = {}
    t_stat_mm = [None, None]
    for kc in range(KCN):
        tsq = kb.comp("scalar", lambda e, kc=kc: e.activation(
            out=sqs, in_=attn[:, kc * TC:(kc + 1) * TC], func=AF.Square,
            bias=0.0, scale=1.0),
            deps=[t_p2[kc], t_sq.get(kc - 1)], sig=True)
        for th in range(2):
            j, bank, bfree = stat_banks[th]
            t_stat_mm[th] = kb.comp(
                "tensor", lambda e, kc=kc, th=th, bank=bank: e.matmul(
                    bank, ones_bf,
                    sqs[:, th * 512:(th + 1) * 512],
                    start=(kc == 0), stop=(kc == KCN - 1)),
                deps=[tsq, bfree if kc == 0 else None, t_ones],
                sig=True)
        t_sq[kc] = t_stat_mm[1]
    t_rstd = []
    for th in range(2):
        j, bank, bfree = stat_banks[th]
        tr = kb.comp("scalar", lambda e, th=th, bank=bank: e.activation(
            out=rstd_rowf[:, th * 512:(th + 1) * 512], in_=bank[0:1, :],
            func=AF.Sqrt, bias=eps6[0:1, :], scale=1.0 / D),
            deps=[t_stat_mm[th]], sig=True)
        fb.release(j, tr)
        t_rstd.append(tr)
    t_rec = kb.comp("vector", lambda e: e.reciprocal(
        out=rstd_rowf, in_=rstd_rowf), deps=list(t_rstd), sig=True)
    t_rcast = kb.comp("scalar", lambda e: e.copy(
        out=rstd_rowb, in_=rstd_rowf), deps=[t_rec], sig=True)
    t_rd = kb.dma("gpsimd", rstd_dram.rearrange("(o n) -> o n", o=1), rstd_rowb,
                  deps=[t_rcast])
    dump("rstdrow", rstd_rowf, [1, TC])
    t_brstd = kb.dma("gpsimd", B_rstd,
                     rstd_dram.rearrange("(o n) -> o n", o=1)
                     .partition_broadcast(128)[:, 0, :],
                     deps=[t_rd, t_p2[H - 1]])

    t_epi = {}
    prev_t1b = None
    for kc in range(KCN):
        ta = kb.comp("vector", lambda e, kc=kc: e.tensor_mul(
            out=t1b, in0=attn[:, kc * TC:(kc + 1) * TC], in1=B_rstd),
            deps=[t_brstd, t_p2[kc], prev_t1b, t_sq.get(KCN - 1)], sig=True)
        tb = kb.comp("vector", lambda e, kc=kc: e.scalar_tensor_tensor(
            out=epi[:, kc * TC:(kc + 1) * TC], in0=t1b,
            scalar=normw_col[:, kc:kc + 1], in1=gate[:, kc * TC:(kc + 1) * TC],
            op0=ALU.mult, op1=ALU.mult),
            deps=[ta, t_nw, t_gate_ev[(kc, 0)], t_gate_ev[(kc, 1)],
                  t_pass2_pe, t_kvin[H - 1]],
            sig=True)
        prev_t1b = tb
        t_epi[kc] = tb
    t_epi_done = kb.fence("sV")

    dump("epi", epi, [128, KCN * TC])

    t_x1ev = {}
    wob_free = None
    prev_tev = None
    for fh in range(2):
        t_conv = {}
        for kc in range(KCN):
            twl = kb.dma("gpsimd", woutb[:, kc * 1024:(kc + 1) * 1024],
                         wout[fh, kc],
                         deps=[t_epi_done if (fh == 0 and kc < 2) else None,
                               wob_free],
                         sem=f"wo{kc % 2}")
            t_conv[kc] = twl
        if fh == 0:
            t_bga = kb.dma("gpsimd", B_ga, bcast(rows_dram.rearrange("a b -> (a b)"),
                                                 0, D), deps=[t_garow, t_epi_done])
        mm_last = None
        for tc in range(NTC):
            for fg in range(2):
                j, bank, bfree = fb.get()
                for kc in range(KCN):
                    mm_last = kb.comp(
                        "tensor",
                        lambda e, fh=fh, tc=tc, fg=fg, kc=kc, bank=bank: e.matmul(
                            bank,
                            epi[:, kc * TC + tc * 128:kc * TC + (tc + 1) * 128],
                            woutb[:, kc * 1024 + fg * 512:
                                  kc * 1024 + fg * 512 + 512],
                            start=(kc == 0), stop=(kc == KCN - 1)),
                        deps=[t_conv[kc], t_epi[kc], bfree],
                        sig=(kc == KCN - 1))
                fcol = fh * 1024 + fg * 512
                tt = kb.comp("vector", lambda e, fcol=fcol, bank=bank:
                             e.tensor_mul(out=silus, in0=bank,
                                          in1=B_ga[:, fcol:fcol + 512]),
                             deps=[mm_last, t_bga, prev_tev], sig=True)
                tev = kb.comp("vector", lambda e, tc=tc, fcol=fcol:
                              e.tensor_add(
                                  out=x1[:, tc * D + fcol:tc * D + fcol + 512],
                                  in0=x1[:, tc * D + fcol:tc * D + fcol + 512],
                                  in1=silus),
                              deps=[tt, t_x1[-1]], sig=True)
                fb.release(j, tev)
                t_x1ev[(fh, tc, fg)] = tev
                prev_tev = tev
        wob_free = mm_last
    t_wout_done = kb.fence("sP")

    dump("x1a", x1, [128, NTC * D])

    t_stat2 = []
    t_ev_all = t_x1ev[(1, NTC - 1, 1)]
    for tc in range(NTC):
        deps = [t_x1ev[(fh, tc, fg)] for fh in range(2) for fg in range(2)]
        deps.append(t_ev_all)
        tid = kb.comp("scalar", lambda e, tc=tc: e.activation(
            out=x1[:, tc * D:(tc + 1) * D],
            in_=x1[:, tc * D:(tc + 1) * D], func=AF.Identity,
            bias=0.0, scale=1.0, accum_out=cols[:, tc:tc + 1]),
            deps=deps + [t_wout_done], sig=True)
        t_stat2.append(kb.comp("scalar", lambda e, tc=tc: e.activation(
            out=t1, in_=x1[:, tc * D:(tc + 1) * D], func=AF.Square,
            bias=0.0, scale=1.0, accum_out=cols[:, 8 + tc:9 + tc]),
            deps=deps + [tid], sig=True))
    t_m8b = kb.comp("vector", lambda e: e.tensor_scalar_mul(
        out=cols[:, 16:24], in0=cols[:, 0:8], scalar1=1.0 / D),
        deps=[t_stat2[-1]], sig=True)
    t_msqb = kb.comp("scalar", lambda e: e.activation(
        out=cols[:, 24:32], in_=cols[:, 16:24], func=AF.Square, bias=0.0,
        scale=1.0), deps=[t_m8b], sig=True)
    t_varb = kb.comp("vector", lambda e: e.scalar_tensor_tensor(
        out=cols[:, 32:40], in0=cols[:, 8:16], scalar=1.0 / D,
        in1=cols[:, 24:32], op0=ALU.mult, op1=ALU.subtract),
        deps=[t_msqb], sig=True)
    t_sdb = kb.comp("scalar", lambda e: e.activation(
        out=cols[:, 48:56], in_=cols[:, 32:40], func=AF.Sqrt, bias=epscol,
        scale=1.0), deps=[t_varb], sig=True)
    t_r8b = kb.comp("vector", lambda e: e.reciprocal(
        out=cols[:, 40:48], in_=cols[:, 48:56]), deps=[t_sdb], sig=True)

    t_b1cb = kb.dma("gpsimd", B_1cb, bcast(agm_flat, 4 * D, 5 * D),
                    deps=[t_agm, t_wout_done, t_epi_done])
    t_b1cb = kb.comp("vector", lambda e: e.tensor_scalar_add(
        out=B_1cb, in0=B_1cb, scalar1=1.0), deps=[t_b1cb], sig=True)
    t_bsb = kb.dma("gpsimd", B_sb, bcast(agm_flat, 3 * D, 4 * D), deps=[t_agm])
    t_binv = kb.dma("gpsimd", B_invgb,
                    bcast(rows_dram.rearrange("a b -> (a b)"), 2 * D, 3 * D),
                    deps=[t_invrow, t_wout_done])

    sc_banks = [fb.get(), fb.get()]
    prev_tr2 = None
    t_h2 = {}
    t_div = {}
    t_sc_mm = [None, None]
    for tc in range(NTC):
        th = tc // 4
        tts = kb.comp("vector", lambda e, tc=tc: e.tensor_scalar(
            out=t1, in0=x1[:, tc * D:(tc + 1) * D],
            scalar1=cols[:, 16 + tc:17 + tc], scalar2=cols[:, 40 + tc:41 + tc],
            op0=ALU.subtract, op1=ALU.mult),
            deps=[t_r8b, prev_tr2], sig=True)
        tt1 = kb.comp("vector", lambda e: e.tensor_mul(
            out=t1, in0=t1, in1=B_1cb), deps=[tts, t_b1cb], sig=True)
        tt2 = kb.comp("vector", lambda e: e.tensor_add(
            out=t1, in0=t1, in1=B_sb), deps=[tt1, t_bsb], sig=True)
        for kc in range(KCN):
            j, bank, bfree = fb.get()
            ttr = kb.comp("tensor", lambda e, kc=kc, bank=bank: e.transpose(
                bank[:, 0:128], t1[:, kc * 128:(kc + 1) * 128], ident32),
                deps=[tt2, bfree], sig=True)
            tev = kb.comp("scalar", lambda e, tc=tc, kc=kc, bank=bank: e.copy(
                out=h2T[:, kc * TC + tc * 128:kc * TC + (tc + 1) * 128],
                in_=bank[:, 0:128]), deps=[ttr], sig=True)
            fb.release(j, tev)
            prev_tr2 = ttr
            t_h2[(tc, kc)] = tev
        t_div[tc] = kb.comp("vector", lambda e, tc=tc: e.tensor_mul(
            out=x1[:, tc * D:(tc + 1) * D], in0=x1[:, tc * D:(tc + 1) * D],
            in1=B_invgb), deps=[tts, t_binv,
                                t_stat2[-1]], sig=True)

    t_gw = kb.dma("gpsimd", gatew_sb, gatew, sem="gw",
                  deps=[t_wout_done, t_epi_done])
    t_sc_ev = []
    for th in range(2):
        j, bank, bfree = sc_banks[th]
        for kc in range(KCN):
            t_sc_mm[th] = kb.comp(
                "tensor", lambda e, kc=kc, th=th, bank=bank: e.matmul(
                    bank,
                    gatew_sb[:, kc * 128:(kc + 1) * 128],
                    h2T[:, kc * TC + th * 512:kc * TC + th * 512 + 512],
                    start=(kc == 0), stop=(kc == KCN - 1)),
                deps=[t_gw, bfree if kc == 0 else None,
                      t_h2[(th * 4, kc)], t_h2[(th * 4 + 1, kc)],
                      t_h2[(th * 4 + 2, kc)], t_h2[(th * 4 + 3, kc)]],
                sig=(kc == KCN - 1))
        tev = kb.comp("scalar", lambda e, th=th, bank=bank: e.copy(
            out=scorest[:, th * 512:(th + 1) * 512], in_=bank[0:8, 0:512]),
            deps=[t_sc_mm[th]], sig=True)
        fb.release(j, tev)
        t_sc_ev.append(tev)

    t_comb_tok = {}
    prev_soft = None
    for tc in range(NTC):
        th = tc // 4
        j, bank, bfree = fb.get()
        ttr = kb.comp("tensor", lambda e, tc=tc, bank=bank: e.transpose(
            bank[:, 0:8], scorest[0:8, tc * 128:(tc + 1) * 128],
            ident32[0:8, 0:8]), deps=[t_sc_ev[th], bfree], sig=True)
        tcp = kb.comp("vector", lambda e, bank=bank: e.tensor_copy(
            out=sctok, in_=bank[:, 0:8]), deps=[ttr, prev_soft], sig=True)
        fb.release(j, tcp)
        dump(f"sctok{tc}", sctok, [128, 8])
        tm1 = kb.comp("vector", lambda e: e.reduce_max(
            out=smt[:, 0:1], in_=sctok, axis=AXX), deps=[tcp], sig=True)
        tneg = kb.comp("vector", lambda e: e.tensor_scalar_mul(
            out=smt[:, 1:2], in0=smt[:, 0:1], scalar1=-1.0), deps=[tm1], sig=True)
        texp = kb.comp("scalar", lambda e: e.activation(
            out=smt[:, 8:16], in_=sctok, func=AF.Exp, bias=smt[:, 1:2],
            scale=1.0, accum_out=smt[:, 2:3]), deps=[tneg], sig=True)
        trec = kb.comp("vector", lambda e: e.reciprocal(
            out=smt[:, 3:4], in_=smt[:, 2:3]), deps=[texp], sig=True)
        tsm = kb.comp("vector", lambda e: e.tensor_scalar(
            out=smt[:, 8:16], in0=smt[:, 8:16], scalar1=smt[:, 3:4],
            scalar2=None, op0=ALU.mult), deps=[trec], sig=True)
        tmx = kb.comp("vector", lambda e: e.reduce_max(
            out=smt[:, 4:5], in_=smt[:, 8:16], axis=AXX), deps=[tsm], sig=True)
        tm1m = kb.comp("vector", lambda e: e.tensor_scalar(
            out=smt[:, 16:24], in0=smt[:, 8:16], scalar1=smt[:, 4:5],
            scalar2=None, op0=ALU.is_ge), deps=[tmx], sig=True)
        ts2 = kb.comp("vector", lambda e: e.scalar_tensor_tensor(
            out=smt[:, 24:32], in0=smt[:, 16:24], scalar=-1e30,
            in1=smt[:, 8:16], op0=ALU.mult, op1=ALU.add), deps=[tm1m], sig=True)
        tmx2 = kb.comp("vector", lambda e: e.reduce_max(
            out=smt[:, 5:6], in_=smt[:, 24:32], axis=AXX), deps=[ts2], sig=True)
        tm2m = kb.comp("vector", lambda e: e.tensor_scalar(
            out=smt[:, 24:32], in0=smt[:, 24:32], scalar1=smt[:, 5:6],
            scalar2=None, op0=ALU.is_ge), deps=[tmx2], sig=True)
        tmsk = kb.comp("vector", lambda e: e.tensor_add(
            out=smt[:, 16:24], in0=smt[:, 16:24], in1=smt[:, 24:32]),
            deps=[tm2m], sig=True)
        prev_soft = kb.comp("vector", lambda e, tc=tc: e.tensor_mul(
            out=comb[:, tc * 8:(tc + 1) * 8], in0=smt[:, 8:16],
            in1=smt[:, 16:24]), deps=[tmsk], sig=True)
        t_comb_tok[tc] = prev_soft

    dump("h2T", h2T, [128, KCN * TC])
    dump("scores", scorest, [8, TC])
    dump("comb", comb, [128, 64])
    dump("x1b", x1, [128, NTC * D])

    w13_free = [None, None]
    w2_free = [None, None]
    t_gff_last = None
    last_expert_mm = kb.fence("sP")
    x1_dep = {tc: [t_div[tc]] for tc in range(NTC)}
    scorest_dead = [t_sc_ev[0], t_sc_ev[1], t_comb_tok[NTC - 1]]

    for ex in range(E + 1):
        shared = (ex == 0)
        w13src = ws13 if shared else w13[ex - 1]
        w2src = ws2d if shared else w2d[ex - 1]
        t_g = {}
        for ic in range(ICN):
            mm_tok = {}
            bank_of = {}
            for m in range(2):
                slot = m
                twl = kb.dma("gpsimd", w13b[slot], w13src[ic, m],
                             deps=[w13_free[slot]] +
                             ([t_wout_done, t_div[NTC - 1]]
                              if (ex == 0 and ic == 0) else []),
                             sem=f"f{slot}")
                for th in range(2):
                    j, bank, bfree = fb.get()
                    for kc in range(KCN):
                        tok = kb.comp(
                            "tensor",
                            lambda e, slot=slot, kc=kc, th=th, bank=bank:
                            e.matmul(
                                bank,
                                w13b[slot][:, kc * 128:(kc + 1) * 128],
                                h2T[:, kc * TC + th * 512:kc * TC + th * 512 + 512],
                                start=(kc == 0), stop=(kc == KCN - 1)),
                            deps=[twl, bfree,
                                  t_h2[(th * 4 + 3, kc)], t_h2[(th * 4, kc)],
                                  t_h2[(th * 4 + 1, kc)], t_h2[(th * 4 + 2, kc)]],
                            sig=(kc == KCN - 1))
                    mm_tok[(m, th)] = tok
                    bank_of[(m, th)] = (j, bank)
                w13_free[slot] = mm_tok[(m, 1)]
            for th in range(2):
                j1, bankA = bank_of[(0, th)]
                j3, bankB = bank_of[(1, th)]
                tsl = kb.comp("scalar", lambda e, bankA=bankA: e.activation(
                    out=silus, in_=bankA, func=AF.Silu, bias=0.0, scale=1.0),
                    deps=[mm_tok[(0, th)], t_gff_last], sig=True)
                t_gff_last = kb.comp("vector",
                                     lambda e, ic=ic, th=th, bankB=bankB:
                                     e.tensor_mul(
                                         out=gff[:, ic * TC + th * 512:
                                                 ic * TC + th * 512 + 512],
                                         in0=silus, in1=bankB),
                                     deps=[tsl, mm_tok[(1, th)],
                                           last_expert_mm], sig=True)
                fb.release(j1, tsl)
                fb.release(j3, t_gff_last)
                t_g[(ic, th)] = t_gff_last
        for fg in range(4):
            extra = ([t_wout_done] if (ex == 0 and fg == 0) else [])
            tw2a = kb.dma("gpsimd", w2b[0].rearrange("p (i f) -> p i f", i=4),
                          w2src[fg, 0:4].transpose([1, 0, 2]),
                          deps=[w2_free[0]] + extra, sem="u0")
            tw2b = kb.dma("gpsimd", w2b[1].rearrange("p (i f) -> p i f", i=4),
                          w2src[fg, 4:8].transpose([1, 0, 2]),
                          deps=[w2_free[1]] + extra, sem="u1")
            for tc in range(NTC):
                j, bank, bfree = fb.get()
                last = None
                for ic in range(ICN):
                    half, i4 = divmod(ic, 4)
                    tw = tw2a if half == 0 else tw2b
                    last = kb.comp(
                        "tensor",
                        lambda e, ic=ic, tc=tc, half=half, i4=i4, bank=bank:
                        e.matmul(
                            bank,
                            gff[:, ic * TC + tc * 128:ic * TC + (tc + 1) * 128],
                            w2b[half][:, i4 * 512:(i4 + 1) * 512],
                            start=(ic == 0), stop=(ic == ICN - 1)),
                        deps=[tw, bfree, t_g[(ic, tc // 4)]],
                        sig=(ic == 3 or ic == ICN - 1))
                    if ic == 3:
                        w2_free[0] = last
                    if ic == ICN - 1:
                        w2_free[1] = last
                if shared:
                    tev = kb.comp("vector", lambda e, tc=tc, fg=fg, bank=bank:
                                  e.tensor_add(
                                      out=x1[:, tc * D + fg * 512:tc * D + fg * 512 + 512],
                                      in0=x1[:, tc * D + fg * 512:tc * D + fg * 512 + 512],
                                      in1=bank),
                                  deps=[last] + x1_dep[tc], sig=True)
                else:
                    tev = kb.comp("vector",
                                  lambda e, tc=tc, fg=fg, ex=ex, bank=bank:
                                  e.scalar_tensor_tensor(
                                      out=x1[:, tc * D + fg * 512:tc * D + fg * 512 + 512],
                                      in0=bank,
                                      scalar=comb[:, tc * 8 + ex - 1:tc * 8 + ex],
                                      in1=x1[:, tc * D + fg * 512:tc * D + fg * 512 + 512],
                                      op0=ALU.mult, op1=ALU.add),
                                  deps=[last, t_comb_tok[tc]] + x1_dep[tc],
                                  sig=True)
                fb.release(j, tev)
                x1_dep[tc] = [tev]
        last_expert_mm = kb.fence("sP")

    dump("x1c", x1, [128, NTC * D])

    t_bgb = kb.dma("gpsimd", B_gb,
                   bcast(rows_dram.rearrange("a b -> (a b)"), 1 * D, 2 * D),
                   deps=[t_gbrow, last_expert_mm, t_gff_last])
    final = []
    for tc in range(NTC):
        tfin = kb.comp("vector", lambda e, tc=tc: e.tensor_mul(
            out=x1[:, tc * D:(tc + 1) * D], in0=x1[:, tc * D:(tc + 1) * D],
            in1=B_gb), deps=[t_bgb] + x1_dep[tc], sig=True)
        final.append(kb.dma("scalar", y_out[tc * 128:(tc + 1) * 128, :],
                            x1[:, tc * D:(tc + 1) * D], deps=[tfin]))
    kb.op("scalar", lambda e: e.nop(), deps=[final[-1]])
    kb.op("gpsimd", lambda e: e.nop(), deps=[kb.fence("sG")])

    with ExitStack() as stack:
        sems = {n: stack.enter_context(nc.semaphore(n)) for n in kb.cnt}
        with nc.Block() as block:
            def mk(name):
                def body(eng):
                    for waits, fn, sig, inc in kb.q[name]:
                        for (s, v) in waits:
                            eng.wait_ge(sems[s], v)
                        ins = fn(eng)
                        if sig is not None:
                            ins.then_inc(sems[sig], inc)
                return body
            block.sync(mk("sync"))
            block.tensor(mk("tensor"))
            block.vector(mk("vector"))
            block.scalar(mk("scalar"))
            block.gpsimd(mk("gpsimd"))
    return nc


_NC_CACHE = {}


def kernel(L0_attn_hidden_states, L0_attn_slope_rate, L0_attn_w_qkv,
           L0_attn_w_gate, L0_attn_w_out, L0_attn_norm_w, L0_ffn_gate_w,
           L0_ffn_w1, L0_ffn_w2, L0_ffn_w3, L0_ffn_ws1, L0_ffn_ws2,
           L0_ffn_ws3, L0_c, L0_wmod):
    w = _prep_weights(L0_attn_w_qkv, L0_attn_w_gate, L0_attn_w_out, L0_wmod,
                      L0_ffn_gate_w, L0_ffn_w1, L0_ffn_w2, L0_ffn_w3,
                      L0_ffn_ws1, L0_ffn_ws2, L0_ffn_ws3)
    x = np.ascontiguousarray(np.asarray(L0_attn_hidden_states,
                                        dtype=np.float32).reshape(T, D))
    slope_a = np.ascontiguousarray(
        np.asarray(L0_attn_slope_rate, dtype=np.float32).reshape(H, 1))
    cvec_a = np.ascontiguousarray(np.asarray(L0_c, dtype=np.float32).reshape(D))
    normw_a = np.ascontiguousarray(
        np.asarray(L0_attn_norm_w, dtype=np.float32).reshape(D))
    mask = _mask01()
    ident = np.eye(128, dtype=np.float32)

    if "nc" not in _NC_CACHE:
        _NC_CACHE["nc"] = build_nc()
    nc = _NC_CACHE["nc"]

    in_maps = []
    for c in range(NC):
        in_maps.append({
            "x_in": np.ascontiguousarray(x[c * TC:(c + 1) * TC]),
            "wqkv": w["wqkv"], "wgate": w["wgate"], "wout": w["wout"],
            "wmod": w["wmod_percore"][c], "gatew": w["gatew"],
            "w13": w["w13"], "ws13": w["ws13"], "w2d": w["w2"], "ws2d": w["ws2"],
            "slope": slope_a, "cvec": cvec_a, "normw": normw_a,
            "dkconst": _dk_const(c), "mask01d": mask, "identd": ident,
        })
    res = run_bass_kernel_spmd(nc, in_maps, core_ids=list(range(NC)))
    out = np.concatenate([res.results[c]["y_out"] for c in range(NC)], axis=0)
    return out.reshape(1, T, D)


# version 20 (final)
# speedup vs baseline: 1.0453x; 1.0453x over previous
import numpy as np
from contextlib import ExitStack

import concourse.bass as bass
import concourse.mybir as mybir
from concourse.bass_utils import run_bass_kernel_spmd
from concourse.replica_groups import maybe_share_collective_output_space

F32 = mybir.dt.float32
F32R = mybir.dt.float32r
BF16 = mybir.dt.bfloat16
U8 = mybir.dt.uint8
AF = mybir.ActivationFunctionType
ALU = mybir.AluOpType
AXX = mybir.AxisListType.X

NC = 8
T = 8192
TC = T // NC
D = 2048
H = 16
DH = 128
BLK = 256
NBLK = TC // BLK
KCN = D // 128
NTC = TC // 128
E = 8
IFF = 1024
ICN = IFF // 128
MODFC = 6 * D // NC // 128
MODF = MODFC * 128
DKC = 784
LNEPS = 1e-5
RMSEPS = 1e-6


class KB:
    ENGS = ("sync", "tensor", "vector", "scalar", "gpsimd")
    DMA_SEM = {"sync": "sD", "scalar": "sA", "gpsimd": "sG"}
    CMP_SEM = {"tensor": "sP", "vector": "sV", "scalar": "sS", "gpsimd": "sGc"}

    def __init__(self):
        self.q = {e: [] for e in self.ENGS}
        self.cnt = {}
        for s in list(self.DMA_SEM.values()) + list(self.CMP_SEM.values()) + ["sCC"]:
            self.cnt[s] = 0
        self.seen = {e: {} for e in self.ENGS}

    def op(self, eng, fn, deps=(), sig=None, inc=1):
        waits = []
        for dtok in deps:
            if dtok is None:
                continue
            s, v = dtok
            if v > self.seen[eng].get(s, 0):
                self.seen[eng][s] = v
                waits.append((s, v))
        tok = None
        if sig is not None:
            if sig not in self.cnt:
                self.cnt[sig] = 0
            self.cnt[sig] += inc
            tok = (sig, self.cnt[sig])
        self.q[eng].append((waits, fn, sig, inc))
        return tok

    def dma(self, eng, out, in_, deps=(), sem=None, chain=True, **kw):
        sem = sem or self.DMA_SEM[eng]
        if sem not in self.cnt:
            self.cnt[sem] = 0
        deps = list(deps)
        if chain and self.cnt[sem] > 0:
            deps.append((sem, self.cnt[sem]))
        return self.op(eng, lambda e: e.dma_start(out=out, in_=in_, **kw),
                       deps, sem, 16)

    def comp(self, eng, fn, deps=(), sig=False):
        return self.op(eng, fn, deps, self.CMP_SEM[eng] if sig else None, 1)

    def fence(self, sem):
        return (sem, self.cnt.get(sem, 0))


def _prep_weights(w_qkv, w_gate, w_out, wmod_, gate_w, w1, w2, w3, ws1, ws2, ws3):
    out = {}
    w = np.asarray(w_qkv, dtype=np.float32).reshape(3, H, DH, KCN, 128)
    out["wqkv"] = np.ascontiguousarray(w.transpose(1, 0, 4, 3, 2)).reshape(H, 3, 128, D)
    g = np.asarray(w_gate, dtype=np.float32).reshape(KCN, 128, KCN, 128)
    out["wgate"] = np.ascontiguousarray(g.transpose(0, 3, 2, 1)).reshape(KCN, 128, D)
    wo = np.asarray(w_out, dtype=np.float32).T.reshape(KCN, 128, 2, 1024)
    out["wout"] = np.ascontiguousarray(wo.transpose(2, 0, 1, 3))
    wm = np.asarray(wmod_, dtype=np.float32).reshape(KCN, 128, 96, 128)
    out["wmod_percore"] = [
        np.ascontiguousarray(wm[:, :, c * MODFC:(c + 1) * MODFC, :]
                             .transpose(2, 1, 0, 3)).reshape(MODFC, 128, D)
        for c in range(NC)]
    gw = np.asarray(gate_w, dtype=np.float32).reshape(E, KCN, 128)
    gwp = np.zeros((128, KCN, 128), dtype=np.float32)
    gwp[:, :, 0:E] = gw.transpose(2, 1, 0)
    out["gatew"] = np.ascontiguousarray(gwp).reshape(128, KCN * 128)
    w1r = np.asarray(w1, dtype=np.float32).reshape(E, ICN, 128, KCN, 128)
    w3r = np.asarray(w3, dtype=np.float32).reshape(E, ICN, 128, KCN, 128)
    out["w13"] = np.ascontiguousarray(
        np.stack([w1r.transpose(0, 1, 4, 3, 2), w3r.transpose(0, 1, 4, 3, 2)],
                 axis=2)).reshape(E, ICN, 2, 128, D)
    s1 = np.asarray(ws1, dtype=np.float32).reshape(ICN, 128, KCN, 128)
    s3 = np.asarray(ws3, dtype=np.float32).reshape(ICN, 128, KCN, 128)
    out["ws13"] = np.ascontiguousarray(
        np.stack([s1.transpose(0, 3, 2, 1), s3.transpose(0, 3, 2, 1)], axis=1)).reshape(ICN, 2, 128, D)
    w2t = np.asarray(w2, dtype=np.float32).transpose(0, 2, 1).reshape(
        E, ICN, 128, 4, 512)
    out["w2"] = np.ascontiguousarray(w2t.transpose(0, 3, 1, 2, 4))
    ws2t = np.asarray(ws2, dtype=np.float32).T.reshape(ICN, 128, 4, 512)
    out["ws2"] = np.ascontiguousarray(ws2t.transpose(2, 0, 1, 3))
    return out


def _dk_const(core):
    row = np.zeros((DKC,), dtype=np.float32)
    row[0:256] = np.arange(1, 257)
    row[256:512] = 255.0 - np.arange(256)
    row[512:768] = -np.arange(1, 257)
    row[768:771] = (256.0, 512.0, 768.0)
    for j in range(NC):
        row[771 + j] = 1024.0 * (core - 1 - j) if j < core else 3.0e5
    return np.ascontiguousarray(np.broadcast_to(row, (H, DKC)))


def _mask01():
    m = np.zeros((128, 512), dtype=np.float32)
    for c in range(2):
        jj = c * 128 + np.arange(128)[:, None]
        ii = np.arange(256)[None, :]
        m[:, c * 256:(c + 1) * 256] = (ii >= jj).astype(np.float32)
    return m


def build_nc(debug=False):
    nc = bass.Bass("TRN2", target_bir_lowering=False)
    kb = KB()

    x_in = nc.dram_tensor("x_in", [TC, D], F32, kind="ExternalInput").ap()
    wqkv = nc.dram_tensor("wqkv", [H, 3, 128, D], F32, kind="ExternalInput").ap()
    wgate = nc.dram_tensor("wgate", [KCN, 128, D], F32, kind="ExternalInput").ap()
    wout = nc.dram_tensor("wout", [2, KCN, 128, 1024], F32, kind="ExternalInput").ap()
    wmod = nc.dram_tensor("wmod", [MODFC, 128, D], F32, kind="ExternalInput").ap()
    gatew = nc.dram_tensor("gatew", [128, KCN * 128], F32, kind="ExternalInput").ap()
    w13 = nc.dram_tensor("w13", [E, ICN, 2, 128, D], F32, kind="ExternalInput").ap()
    ws13 = nc.dram_tensor("ws13", [ICN, 2, 128, D], F32, kind="ExternalInput").ap()
    w2d = nc.dram_tensor("w2d", [E, 4, ICN, 128, 512], F32, kind="ExternalInput").ap()
    ws2d = nc.dram_tensor("ws2d", [4, ICN, 128, 512], F32, kind="ExternalInput").ap()
    slope = nc.dram_tensor("slope", [H, 1], F32, kind="ExternalInput").ap()
    cvec = nc.dram_tensor("cvec", [D], F32, kind="ExternalInput").ap()
    normw = nc.dram_tensor("normw", [D], F32, kind="ExternalInput").ap()
    dkconst = nc.dram_tensor("dkconst", [H, DKC], F32, kind="ExternalInput").ap()
    mask01d = nc.dram_tensor("mask01d", [128, 512], F32, kind="ExternalInput").ap()
    identd = nc.dram_tensor("identd", [128, 128], F32, kind="ExternalInput").ap()
    y_out = nc.dram_tensor("y_out", [TC, D], F32, kind="ExternalOutput").ap()

    rg = [list(range(NC))]
    sp = maybe_share_collective_output_space("AllGather", rg, prefer_local=True)
    agm_in = nc.dram_tensor("agm_in", [MODF], F32).ap()
    agm_out = nc.dram_tensor("agm_out", [NC, MODF], F32, addr_space=sp).ap()
    agk_in = nc.dram_tensor("agk_in", [H, 128, 128], F32).ap()
    agk_out = nc.dram_tensor("agk_out", [NC, H, 128, 128], F32, addr_space=sp).ap()
    dk_dram = nc.dram_tensor("dk_dram", [H, DKC], F32).ap()
    dk16_dram = nc.dram_tensor("dk16_dram", [H * 16], F32).ap()
    rows_dram = nc.dram_tensor("rows_dram", [4, D], F32).ap()
    rstd_dram = nc.dram_tensor("rstd_dram", [TC], BF16).ap()

    TOTAL = 212448
    backing = nc.alloc_sbuf_tensor("arena", [128, TOTAL], U8)
    base = nc.lookup_mloc(backing).addr
    made = {}

    def sb(name, shape, dtype, off):
        assert name not in made
        nbytes = int(np.prod(shape[1:])) * mybir.dt.size(dtype)
        assert off + nbytes <= TOTAL, f"{name}: {off}+{nbytes} > {TOTAL}"
        h = nc.virtual_sbuf_tensor(name, list(shape), dtype)
        nc._tensor(h.name, list(shape), dtype, type="SB")
        mloc = nc.lookup_mloc(h)
        mloc.allocated = True
        mloc.addr = base + off
        made[name] = (off, nbytes)
        return h.ap()

    ident32 = sb("ident32", [128, 128], F32, 0)
    identb = sb("identb", [128, 128], BF16, 512)
    O_A1 = 800
    h1T = sb("h1T", [128, KCN * TC], BF16, O_A1)
    x1 = sb("x1", [128, NTC * D], F32, O_A1)
    O_A25 = O_A1 + 65536
    xtok = sb("xtok", [128, NTC * D], F32, O_A25)
    attn = sb("attn", [128, H * TC], BF16, O_A25)
    epi = sb("epi", [128, KCN * TC], BF16, O_A25 + 32768)
    h2T = sb("h2T", [128, KCN * TC], BF16, O_A25)
    O_HI = O_A25 + 32768
    qkvh = [sb(f"qkvh{i}", [128, 3 * TC], BF16, O_HI + i * 6144) for i in range(2)]
    wqkvb = [sb(f"wqkvb{i}", [128, 1024], BF16, O_A1 + 32768 + i * 2048)
             for i in range(4)]
    kvf = sb("kvf", [128, H * 128], F32, O_HI + 20480)
    kvbf = sb("kvbf", [128, H * 128], BF16, O_HI + 28672)
    gatebuf = [sb(f"gatebuf{i}", [128, D], BF16, O_HI + i * 4096) for i in range(2)]
    kvall = sb("kvall", [128, H * 128], F32, O_HI + 8192)
    O_A3 = O_A25 + 65536
    qd = sb("qd", [128, H * TC], BF16, O_A3)
    B_1ca = sb("B_1ca", [128, D], F32, O_A3)
    B_sa = sb("B_sa", [128, D], F32, O_A3 + 8192)
    rowA = sb("rowA", [1, D], F32, O_A3 + 16384)
    rowB = sb("rowB", [1, D], F32, O_A3 + 24576)
    woutb = sb("woutb", [128, KCN * 1024], BF16, O_A3)
    B_rstd = sb("B_rstd", [128, TC], BF16, O_A3 + 32768 + 32768 + 8192 + 3136)
    B_1cb = sb("B_1cb", [128, D], F32, O_A3)
    B_sb = sb("B_sb", [128, D], F32, O_A3 + 8192)
    B_invgb = sb("B_invgb", [128, D], F32, O_A3 + 16384)
    gff = sb("gff", [128, ICN * TC], BF16, O_A3)
    B_gb = sb("B_gb", [128, D], F32, O_A3 + 32768)
    O_A4 = O_A3 + 32768
    gate = sb("gate", [128, KCN * TC], BF16, O_A4)
    wmodb = [sb(f"wmodb{i}", [128, D], F32, O_A4 + i * 8192) for i in range(4)]
    woutstage = [sb(f"woutstage{i}", [128, 1024], F32, O_A4 + i * 4096)
                 for i in range(2)]
    B_ga = sb("B_ga", [128, D], F32, O_A4 + 16384)
    scorest = sb("scorest", [8, TC], F32, O_A4)
    w13b = [sb(f"w13b{i}", [128, D], BF16, O_A3 + 16384 + i * 4096) for i in range(2)]
    w2b = [sb(f"w2b{i}", [128, 2048], BF16, O_A3 + 24576 + i * 4096)
           for i in range(2)]
    O_SCR = O_A4 + 32768
    t1 = sb("t1", [128, D], F32, O_SCR)
    sqs = sb("sqs", [128, TC], BF16, O_SCR)
    t1b = sb("t1b", [128, TC], BF16, O_SCR + 2048)
    rstd_rowf = sb("rstd_rowf", [1, TC], F32, O_SCR + 4096)
    rstd_rowb = sb("rstd_rowb", [1, TC], BF16, O_SCR + 2048)
    mask01 = sb("mask01", [128, 512], F32, O_SCR)
    dmask = [sb(f"dmask{i}", [128, 512], F32, O_SCR + 2048 + i * 2048)
             for i in range(2)]
    aT = [sb(f"aT{i}", [128, 256], BF16, O_SCR + 6144 + i * 512) for i in range(2)]
    qdb = sb("qdb", [128, 256], BF16, O_SCR + 7168)
    p2b = sb("p2b", [128, 256], BF16, O_SCR + 7680)
    silus = sb("silus", [128, 512], F32, O_SCR + 2048)
    O_Z = O_SCR + 8192
    dk_sb = sb("dk_sb", [H, DKC], F32, O_Z)
    qdectall = sb("qdectall", [128, H * 256], BF16, O_A4 + 16384)
    percol = sb("percol", [128, 256], F32, O_Z + 4160)
    O_CP = O_Z + 5184
    cols = sb("cols", [128, 64], F32, O_CP)
    c_fm = sb("c_fm", [128, 16], F32, O_CP + 256)
    c_silu = sb("c_silu", [128, 16], F32, O_CP + 320)
    modcol = sb("modcol", [128, MODFC], F32, O_CP + 384)
    epscol = sb("epscol", [128, 1], F32, O_CP + 448)
    normw_col = sb("normw_col", [128, 16], F32, O_CP + 464)
    ones_bf = sb("ones_bf", [128, 128], BF16, O_CP + 528)
    kdcol = sb("kdcol", [128, 2], F32, O_CP + 800)
    comb = sb("comb", [128, 64], F32, O_CP + 816)
    smt = sb("smt", [128, 40], F32, O_CP + 1088)
    sctok = sb("sctok", [128, 8], F32, O_CP + 1248)
    gatew_sb = sb("gatew_sb", [128, KCN * 128], BF16, O_A4 + 4096)
    s16 = sb("s16", [H, 1], F32, O_CP + 1312)
    negs16 = sb("negs16", [H, 1], F32, O_CP + 1328)
    eps6 = sb("eps6", [128, 1], F32, O_CP + 1344)
    assert O_CP + 1600 <= TOTAL

    ps = [nc.alloc_psum_tensor(f"ps{i}", [128, 512], F32).ap() for i in range(6)]
    pb = [nc.alloc_psum_tensor(f"pb{i}", [128, 1024], BF16).ap() for i in range(2)]

    class BankSet:
        def __init__(self, aps):
            self.aps = aps
            self.avail = [(i, None) for i in range(len(aps))]

        def get(self):
            j, tok = self.avail.pop(0)
            return j, self.aps[j], tok

        def release(self, j, tok):
            self.avail.append((j, tok))

    fb = BankSet(ps)
    bb = BankSet(pb)

    def row1(ap_flat, a, b):
        return ap_flat[a:b].rearrange("(o n) -> o n", o=1)

    def bcast(ap_flat, a, b):
        return row1(ap_flat, a, b).partition_broadcast(128)[:, 0, :]

    def colap(ap_flat, a, b):
        return ap_flat[a:b].rearrange("(p o) -> p o", o=1)

    dbg_outs = {}

    def dump(name, src_ap, shape):
        if not debug:
            return
        d = nc.dram_tensor("dbg_" + name, list(shape), src_ap.dtype,
                           kind="ExternalOutput").ap()
        t = kb.dma("gpsimd", d, src_ap,
                   deps=[kb.fence("sV"), kb.fence("sS"), kb.fence("sP"),
                         kb.fence("sA"), kb.fence("sG")])
        for eng in ("vector", "scalar", "tensor", "gpsimd", "sync"):
            kb.op(eng, lambda e: e.nop(), deps=[t])
        dbg_outs[name] = shape

    spacecol = sb("spacecol", [128, 2], F32, O_CP + 1360)

    def vsp():
        pass

    t_ident = kb.dma("gpsimd", ident32, identd)
    t_identb = kb.comp("vector", lambda e: e.tensor_copy(out=identb, in_=ident32),
                       deps=[t_ident], sig=True)
    t_dkc = kb.dma("gpsimd", dk_sb, dkconst)
    t_slope = kb.dma("gpsimd", s16, slope)
    t_cv = kb.dma("gpsimd", c_fm, cvec.rearrange("(k p) -> p k", p=128),
                  allow_slow_non_contiguous=True)
    t_nw = kb.dma("gpsimd", normw_col, normw.rearrange("(k p) -> p k", p=128),
                  allow_slow_non_contiguous=True)

    kb.comp("vector", lambda e: e.memset(spacecol, 0.0))
    kb.comp("vector", lambda e: e.memset(epscol, LNEPS))
    kb.comp("vector", lambda e: e.memset(eps6, RMSEPS))
    t_ones = kb.comp("vector", lambda e: e.memset(ones_bf, 1.0), sig=True)
    t_negs = kb.comp("vector", lambda e: e.tensor_scalar_mul(
        out=negs16, in0=s16, scalar1=-1.0), deps=[t_slope], sig=True)
    t_dk = kb.comp("scalar", lambda e: e.activation(
        out=dk_sb, in_=dk_sb, func=AF.Exp, bias=0.0, scale=negs16),
        deps=[t_dkc, t_negs], sig=True)
    t_dkd = kb.dma("gpsimd", dk_dram, dk_sb, deps=[t_dk])
    t_dk16 = kb.dma("gpsimd", dk16_dram.rearrange("(a b) -> a b", b=16),
                    dk_sb[:, 768:784], deps=[t_dk])
    t_percol = kb.dma("gpsimd", percol, bcast(dk16_dram, 0, 256),
                      deps=[t_dk16])
    t_csilu = kb.comp("scalar", lambda e: e.activation(
        out=c_silu, in_=c_fm, func=AF.Silu, bias=0.0, scale=1.0),
        deps=[t_cv], sig=True)

    t_x, t_stat = [], []
    for tc in range(NTC):
        tx = kb.dma("scalar", xtok[:, tc * D:(tc + 1) * D],
                    x_in[tc * 128:(tc + 1) * 128, :])
        t_x.append(tx)
        tid = kb.comp("scalar", lambda e, tc=tc: e.activation(
            out=xtok[:, tc * D:(tc + 1) * D],
            in_=xtok[:, tc * D:(tc + 1) * D], func=AF.Identity,
            bias=0.0, scale=1.0, accum_out=cols[:, tc:tc + 1]), deps=[tx],
            sig=True)
        t_stat.append(kb.comp("scalar", lambda e, tc=tc: e.activation(
            out=t1, in_=xtok[:, tc * D:(tc + 1) * D], func=AF.Square,
            bias=0.0, scale=1.0, accum_out=cols[:, 8 + tc:9 + tc]),
            deps=[tid], sig=True))
    t_m8 = kb.comp("vector", lambda e: e.tensor_scalar_mul(
        out=cols[:, 16:24], in0=cols[:, 0:8], scalar1=1.0 / D),
        deps=[t_stat[-1]], sig=True)
    t_msq = kb.comp("scalar", lambda e: e.activation(
        out=cols[:, 24:32], in_=cols[:, 16:24], func=AF.Square, bias=0.0,
        scale=1.0), deps=[t_m8], sig=True)
    t_var = kb.comp("vector", lambda e: e.scalar_tensor_tensor(
        out=cols[:, 32:40], in0=cols[:, 8:16], scalar=1.0 / D,
        in1=cols[:, 24:32], op0=ALU.mult, op1=ALU.subtract),
        deps=[t_msq], sig=True)
    t_sd = kb.comp("scalar", lambda e: e.activation(
        out=cols[:, 48:56], in_=cols[:, 32:40], func=AF.Sqrt, bias=epscol,
        scale=1.0), deps=[t_var], sig=True)
    t_r8 = kb.comp("vector", lambda e: e.reciprocal(
        out=cols[:, 40:48], in_=cols[:, 48:56]), deps=[t_sd], sig=True)

    wm_free = [None, None, None, None]
    for fc in range(MODFC):
        slot = fc % 4
        twl = kb.dma("sync", wmodb[slot], wmod[fc], deps=[wm_free[slot]],
                     sem=f"m{slot}")
        j, bank, bfree = fb.get()
        last = None
        for kc in range(KCN):
            last = kb.comp("tensor", lambda e, slot=slot, kc=kc, bank=bank:
                           e.matmul(bank[:, 0:1],
                                    wmodb[slot][:, kc * 128:(kc + 1) * 128],
                                    c_silu[:, kc:kc + 1],
                                    start=(kc == 0), stop=(kc == KCN - 1)),
                           deps=[twl, t_csilu, bfree],
                           sig=(kc == KCN - 1))
        wm_free[slot] = last
        tev = kb.comp("vector", lambda e, fc=fc, bank=bank: e.tensor_copy(
            out=modcol[:, fc:fc + 1], in_=bank[:, 0:1]), deps=[last], sig=True)
        fb.release(j, tev)
        kb.dma("gpsimd", colap(agm_in, fc * 128, (fc + 1) * 128),
               modcol[:, fc:fc + 1], deps=[tev])
    t_agm = kb.op("gpsimd", lambda e: e.collective_compute(
        "AllGather", ALU.bypass, replica_groups=rg, ins=[agm_in.opt()],
        outs=[agm_out.opt()]), deps=[kb.fence("sG")], sig="sCC", inc=1)

    agm_flat = agm_out.rearrange("a b -> (a b)")
    t_b1ca = kb.dma("gpsimd", B_1ca, bcast(agm_flat, 1 * D, 2 * D), deps=[t_agm])
    t_b1ca = kb.comp("vector", lambda e: e.tensor_scalar_add(
        out=B_1ca, in0=B_1ca, scalar1=1.0), deps=[t_b1ca], sig=True)
    t_bsa = kb.dma("gpsimd", B_sa, bcast(agm_flat, 0, D), deps=[t_agm])

    def clamp_row(src_k, dst_i, deps):
        tload = kb.dma("gpsimd", rowA, row1(agm_flat, src_k * D, (src_k + 1) * D),
                       deps=deps)
        tabs = kb.comp("scalar", lambda e: e.activation(
            out=rowB, in_=rowA, func=AF.Abs, bias=0.0, scale=1.0),
            deps=[tload], sig=True)
        tmax = kb.comp("vector", lambda e: e.tensor_scalar_max(
            out=rowB, in0=rowB, scalar1=0.25), deps=[tabs], sig=True)
        tge = kb.comp("vector", lambda e: e.tensor_scalar(
            out=rowA, in0=rowA, scalar1=0.0, scalar2=None, op0=ALU.is_ge),
            deps=[tload, tabs], sig=True)
        tsgn = kb.comp("vector", lambda e: e.tensor_scalar(
            out=rowA, in0=rowA, scalar1=2.0, scalar2=-1.0, op0=ALU.mult,
            op1=ALU.add), deps=[tge], sig=True)
        tmul = kb.comp("vector", lambda e: e.tensor_mul(
            out=rowA, in0=rowA, in1=rowB), deps=[tsgn, tmax], sig=True)
        tst = kb.dma("gpsimd", rows_dram[dst_i].rearrange("(o n) -> o n", o=1),
                     rowA, deps=[tmul])
        return tst, tmul

    t_garow, _ = clamp_row(2, 0, [t_agm])
    t_gbrow, t_gbmul = clamp_row(5, 1, [t_garow])
    t_invr = kb.comp("vector", lambda e: e.reciprocal(out=rowB, in_=rowA),
                     deps=[t_gbmul], sig=True)
    t_invrow = kb.dma("gpsimd", rows_dram[2].rearrange("(o n) -> o n", o=1),
                      rowB, deps=[t_invr])

    prev_tr = None
    t_h1 = None
    for tc in range(NTC):
        tts = kb.comp("vector", lambda e, tc=tc: e.tensor_scalar(
            out=t1, in0=xtok[:, tc * D:(tc + 1) * D],
            scalar1=cols[:, 16 + tc:17 + tc], scalar2=cols[:, 40 + tc:41 + tc],
            op0=ALU.subtract, op1=ALU.mult),
            deps=[t_r8, t_x[tc], prev_tr], sig=True)
        tt1 = kb.comp("vector", lambda e: e.tensor_mul(
            out=t1, in0=t1, in1=B_1ca), deps=[tts, t_b1ca], sig=True)
        tt2 = kb.comp("vector", lambda e: e.tensor_add(
            out=t1, in0=t1, in1=B_sa), deps=[tt1, t_bsa], sig=True)
        for kc in range(KCN):
            j, bank, bfree = fb.get()
            ttr = kb.comp("tensor", lambda e, kc=kc, bank=bank: e.transpose(
                bank[:, 0:128], t1[:, kc * 128:(kc + 1) * 128], ident32),
                deps=[tt2, bfree, t_ident], sig=True)
            t_h1 = kb.comp("scalar", lambda e, tc=tc, kc=kc, bank=bank: e.copy(
                out=h1T[:, kc * TC + tc * 128:kc * TC + (tc + 1) * 128],
                in_=bank[:, 0:128]), deps=[ttr], sig=True)
            fb.release(j, t_h1)
            prev_tr = ttr

    dump("h1T", h1T, [128, KCN * TC])
    dump("b1ca", B_1ca, [128, D])
    dump("bsa", B_sa, [128, D])
    dump("cols0", cols, [128, 64])

    t_mask = kb.dma("gpsimd", mask01, mask01d, deps=[t_h1, prev_tr])

    kvz1 = kb.comp("vector", lambda e: e.memset(kvf, 0.0),
                   deps=[t_h1], sig=True)
    kvz2 = kb.comp("vector", lambda e: e.memset(kvbf, 0.0),
                   deps=[t_h1], sig=True)

    for h in range(H):
        kb.dma("gpsimd", qdectall[:, h * 256:(h + 1) * 256],
               bcast(dk_dram[h], 0, 256), deps=[t_dkd], sem="qda", chain=False)
    t_qda = kb.fence("qda")

    wq_free = [None, None, None, None]
    slot_free = [None, None]
    dmask_free = [None, None]
    qdec_free = [None, None]
    t_kv_dma = []
    t_qd = {}
    t_p1 = {}
    t_attn_w = {}
    prev_tdm = [None, None]
    prev_outgrp = [None, None]

    for h in range(H):
        slot = h % 2
        qs = qkvh[slot]
        dsem = f"dh{h % 2}"
        tkd0 = kb.dma("sync", kdcol[:, 0:1], colap(dk_dram[h], 256, 384),
                      deps=[t_dkd, t_p1.get(h - 1)], sem=dsem, chain=False)
        tkd1 = kb.dma("sync", kdcol[:, 1:2], colap(dk_dram[h], 384, 512),
                      deps=[t_dkd], sem=dsem, chain=False)
        dslot = dmask[h % 2]
        tdm0 = kb.dma("sync", dslot[:, 0:256], bcast(dk_dram[h], 0, 256),
                      deps=[t_dkd, dmask_free[h % 2], t_mask], sem=dsem,
                      chain=False)
        tdm1 = kb.dma("sync", dslot[:, 256:512], bcast(dk_dram[h], 0, 256),
                      deps=[t_dkd], sem=dsem, chain=False)
        texc = [kb.dma("sync", smt[:, 32 + c:33 + c],
                       colap(dk_dram[h], 512 + c * 128, 512 + (c + 1) * 128),
                       deps=[t_dkd, prev_tdm[c] if h > 0 else None], sem=dsem,
                       chain=False)
                for c in range(2)]
        t_dh = kb.fence(dsem)
        tkd0 = tkd1 = tdm0 = tdm1 = t_dh
        texc = [t_dh, t_dh]
        tqdec = t_qda
        tdm = []
        for c in range(2):
            tt = kb.comp("vector", lambda e, c=c, dslot=dslot: e.tensor_scalar(
                out=dslot[:, c * 256:(c + 1) * 256],
                in0=dslot[:, c * 256:(c + 1) * 256],
                scalar1=smt[:, 32 + c:33 + c], scalar2=None, op0=ALU.mult),
                deps=[tdm0, tdm1, texc[c]], sig=True)
            tt = kb.comp("vector", lambda e, c=c, dslot=dslot: e.tensor_mul(
                out=dslot[:, c * 256:(c + 1) * 256],
                in0=dslot[:, c * 256:(c + 1) * 256],
                in1=mask01[:, c * 256:(c + 1) * 256]),
                deps=[tt, t_mask], sig=True)
            tdm.append(tt)
        prev_tdm = tdm

        part_done = []
        for p in range(3):
            banksets = [fb.get() for _ in range(2)]
            for wh in range(2):
                wsl = (h % 2) * 2 + wh
                ws = wqkvb[wsl]
                twl = kb.dma("gpsimd", ws, wqkv[h, p, :, wh * 1024:(wh + 1) * 1024],
                             deps=[wq_free[wsl]], sem=f"q{wsl}")
                for th in range(2):
                    jj, bank, bfree = banksets[th]
                    for k8 in range(8):
                        kc = wh * 8 + k8
                        tok = kb.comp(
                            "tensor",
                            lambda e, ws=ws, k8=k8, kc=kc, th=th, bank=bank:
                            e.matmul(
                                bank,
                                ws[:, k8 * 128:(k8 + 1) * 128],
                                h1T[:, kc * TC + th * 512:kc * TC + th * 512 + 512],
                                start=(kc == 0), stop=(kc == KCN - 1)),
                            deps=[twl, bfree],
                            sig=(k8 == 7))
                        if k8 == 7:
                            wq_free[wsl] = tok
                            if wh == 1:
                                last_mm = tok
            for th in range(2):
                jj, bank, bfree = banksets[th]
                tev = kb.comp("scalar", lambda e, p=p, th=th, bank=bank, qs=qs:
                              e.activation(
                                  out=qs[:, p * TC + th * 512:p * TC + th * 512 + 512],
                                  in_=bank, func=AF.Silu, bias=0.0, scale=1.0),
                              deps=[last_mm] + (slot_free[slot] or []), sig=True)
                part_done.append(tev)
                fb.release(jj, tev)
        t_q, t_k, t_v = part_done[1], part_done[3], part_done[5]

        tqd = None
        for b in range(NBLK):
            tqd = kb.comp("vector", lambda e, h=h, b=b, qs=qs:
                          e.tensor_mul(
                              out=qd[:, h * TC + b * 256:h * TC + b * 256 + 256],
                              in0=qs[:, 0 * TC + b * 256:0 * TC + b * 256 + 256],
                              in1=qdectall[:, h * 256:(h + 1) * 256]),
                          deps=[t_q, tqdec], sig=True)
        t_qd[h] = tqd

        tv_tok = []
        for tcc in range(NTC):
            j, bank, bfree = bb.get()
            ttr = kb.comp("tensor", lambda e, tcc=tcc, qs=qs, bank=bank:
                          e.transpose(bank[:, 0:128],
                                      qs[:, 2 * TC + tcc * 128:2 * TC + (tcc + 1) * 128],
                                      identb),
                          deps=[t_v, bfree, t_identb], sig=True)
            tev = kb.comp("scalar", lambda e, tcc=tcc, qs=qs, bank=bank:
                          e.copy(out=qs[:, 2 * TC + tcc * 128:2 * TC + (tcc + 1) * 128],
                                 in_=bank[:, 0:128]),
                          deps=[ttr], sig=True)
            bb.release(j, tev)
            tv_tok.append(tev)

        if h == 0:
            dump("qs0", qs, [128, 3 * TC])
            dump("dmask0", dmask[0], [128, 512])
            dump("qdect0", qdectall[:, 0:256], [128, 256])
        t_scan_last = None
        attn_writes = []
        for b in range(NBLK):
            kqev = []
            for kk in range(2):
                j, bank, bfree = fb.get()
                tmm = kb.comp("tensor", lambda e, b=b, kk=kk, qs=qs, bank=bank:
                              e.matmul(
                                  bank[:, 0:256],
                                  qs[:, 1 * TC + b * 256 + kk * 128:
                                      1 * TC + b * 256 + kk * 128 + 128],
                                  qs[:, 0 * TC + b * 256:0 * TC + b * 256 + 256],
                                  start=True, stop=True),
                              deps=[t_q, t_k, bfree], sig=True)
                tat = kb.comp("vector", lambda e, kk=kk, bank=bank, dslot=dslot:
                              e.tensor_mul(out=aT[kk], in0=bank[:, 0:256],
                                           in1=dslot[:, kk * 256:(kk + 1) * 256]),
                              deps=[tmm, tdm[kk], prev_outgrp[kk]], sig=True)
                fb.release(j, tat)
                kqev.append(tat)
            tk_tok = []
            for kk in range(2):
                tcc = b * 2 + kk
                j, bank, bfree = bb.get()
                ttr = kb.comp("tensor", lambda e, tcc=tcc, qs=qs, bank=bank:
                              e.transpose(bank[:, 0:128],
                                          qs[:, 1 * TC + tcc * 128:
                                              1 * TC + (tcc + 1) * 128],
                                          identb),
                              deps=[kqev[1], bfree, t_identb], sig=True)
                tev = kb.comp("vector", lambda e, tcc=tcc, kk=kk, qs=qs, bank=bank:
                              e.tensor_scalar(
                                  out=qs[:, 1 * TC + tcc * 128:1 * TC + (tcc + 1) * 128],
                                  in0=bank[:, 0:128], scalar1=kdcol[:, kk:kk + 1],
                                  scalar2=None, op0=ALU.mult),
                              deps=[ttr, tkd0, tkd1], sig=True)
                bb.release(j, tev)
                tk_tok.append(tev)
            j, bank, bfree = fb.get()
            for kk in range(2):
                kb.comp("tensor", lambda e, b=b, kk=kk, qs=qs, bank=bank:
                        e.matmul(bank[:, 0:256],
                                 qs[:, 2 * TC + (b * 2 + kk) * 128:
                                     2 * TC + (b * 2 + kk + 1) * 128],
                                 aT[kk], start=(kk == 0), stop=False),
                        deps=[kqev[kk], tv_tok[b * 2 + kk], bfree])
            tmm = kb.comp("tensor", lambda e, h=h, b=b, bank=bank:
                          e.matmul(bank[:, 0:256],
                                   kvbf[:, h * 128:(h + 1) * 128],
                                   qd[:, h * TC + b * 256:h * TC + b * 256 + 256],
                                   start=False, stop=True),
                          deps=[t_qd[h], kvz2 if b == 0 else t_scan_last],
                          sig=True)
            tev = kb.comp("scalar", lambda e, h=h, b=b, bank=bank:
                          e.copy(out=attn[:, h * TC + b * 256:h * TC + b * 256 + 256],
                                 in_=bank[:, 0:256]),
                          deps=[tmm], sig=True)
            fb.release(j, tev)
            attn_writes.append(tev)
            prev_outgrp = [tmm, tmm]
            if h == 0 and b == 0:
                dump("aT0", aT[0], [128, 256])
                dump("aT1", aT[1], [128, 256])
                dump("vtok0", qs, [128, 3 * TC])
            j2, bank2, bfree2 = fb.get()
            for kk in range(2):
                tok = kb.comp("tensor", lambda e, b=b, kk=kk, qs=qs, bank2=bank2:
                              e.matmul(bank2[:, 0:128],
                                       qs[:, 1 * TC + (b * 2 + kk) * 128:
                                           1 * TC + (b * 2 + kk + 1) * 128],
                                       qs[:, 2 * TC + (b * 2 + kk) * 128:
                                           2 * TC + (b * 2 + kk + 1) * 128],
                                       start=(kk == 0), stop=(kk == 1)),
                              deps=[tk_tok[kk], tv_tok[b * 2 + kk], bfree2],
                              sig=(kk == 1))
            tupd = kb.comp("vector", lambda e, h=h, bank2=bank2:
                           e.scalar_tensor_tensor(
                               out=kvf[:, h * 128:(h + 1) * 128],
                               in0=kvf[:, h * 128:(h + 1) * 128],
                               scalar=percol[:, h * 16:h * 16 + 1],
                               in1=bank2[:, 0:128], op0=ALU.mult, op1=ALU.add),
                           deps=[tok, kvz1, t_percol], sig=True)
            fb.release(j2, tupd)
            t_scan_last = kb.comp("scalar", lambda e, h=h:
                                  e.copy(out=kvbf[:, h * 128:(h + 1) * 128],
                                         in_=kvf[:, h * 128:(h + 1) * 128]),
                                  deps=[tupd], sig=True)
        t_kv_dma.append(kb.dma("scalar", agk_in[h], kvf[:, h * 128:(h + 1) * 128],
                               deps=[tupd]))
        dmask_free[h % 2] = kqev[1]
        slot_free[slot] = [tupd, t_qd[h], attn_writes[-1]]
        t_p1[h] = tupd
        t_attn_w[h] = attn_writes

    t_agk = kb.op("gpsimd", lambda e: e.collective_compute(
        "AllGather", ALU.bypass, replica_groups=rg, ins=[agk_in.opt()],
        outs=[agk_out.opt()]), deps=[t_kv_dma[-1], t_agm], sig="sCC", inc=1)

    gb_free = [None, None]
    t_gate_mm = None
    t_gate_ev = {}
    for fc in range(KCN):
        slot = fc % 2
        twl = kb.dma("gpsimd", gatebuf[slot], wgate[fc],
                     deps=[gb_free[slot]] + (slot_free[0] or [])
                     + (slot_free[1] or []),
                     sem=f"gt{slot}")
        for th in range(2):
            j, bank, bfree = fb.get()
            for kc in range(KCN):
                t_gate_mm = kb.comp(
                    "tensor",
                    lambda e, slot=slot, kc=kc, th=th, bank=bank: e.matmul(
                        bank, gatebuf[slot][:, kc * 128:(kc + 1) * 128],
                        h1T[:, kc * TC + th * 512:kc * TC + th * 512 + 512],
                        start=(kc == 0), stop=(kc == KCN - 1)),
                    deps=[twl, bfree], sig=(kc == KCN - 1))
            if th == 1:
                gb_free[slot] = t_gate_mm
            tev = kb.comp("scalar", lambda e, fc=fc, th=th, bank=bank:
                          e.activation(
                              out=gate[:, fc * TC + th * 512:fc * TC + th * 512 + 512],
                              in_=bank, func=AF.Sigmoid, bias=0.0, scale=1.0),
                          deps=[t_gate_mm], sig=True)
            t_gate_ev[(fc, th)] = tev
            fb.release(j, tev)
    t_gate_done = kb.fence("sP")

    t_x1 = []
    for tc in range(NTC):
        t_x1.append(kb.dma("scalar", x1[:, tc * D:(tc + 1) * D],
                           x_in[tc * 128:(tc + 1) * 128, :],
                           deps=[t_gate_done]))

    dump("gate", gate, [128, KCN * TC])

    last_gate_reads = []
    t_combh = {h: None for h in range(H)}
    for jrank in range(NC):
        tkv = kb.dma("gpsimd", kvall.rearrange("p (h e) -> p h e", h=H),
                     agk_out[jrank].transpose([1, 0, 2]),
                     deps=[t_agk] + ((slot_free[0] or []) + (slot_free[1] or [])
                                     if jrank == 0 else [t_combh[H - 1]]))
        for h in range(H):
            if jrank == 0:
                tc0 = kb.comp("vector", lambda e, h=h, jrank=jrank: e.tensor_scalar(
                    out=kvf[:, h * 128:(h + 1) * 128],
                    in0=kvall[:, h * 128:(h + 1) * 128],
                    scalar1=percol[:, h * 16 + 3 + jrank:h * 16 + 4 + jrank],
                    scalar2=None, op0=ALU.mult),
                    deps=[tkv, t_percol, t_kv_dma[h], t_p1[H - 1]], sig=True)
            else:
                tc0 = kb.comp("vector", lambda e, h=h, jrank=jrank:
                              e.scalar_tensor_tensor(
                                  out=kvf[:, h * 128:(h + 1) * 128],
                                  in0=kvall[:, h * 128:(h + 1) * 128],
                                  scalar=percol[:, h * 16 + 3 + jrank:h * 16 + 4 + jrank],
                                  in1=kvf[:, h * 128:(h + 1) * 128],
                                  op0=ALU.mult, op1=ALU.add),
                              deps=[tkv, t_combh[h]], sig=True)
            t_combh[h] = tc0
        last_gate_reads = [t_combh[H - 1], t_combh[0], t_combh[7]]
    t_kvin = {}
    for h in range(H):
        t_kvin[h] = kb.comp("scalar", lambda e, h=h:
                            e.copy(out=kvbf[:, h * 128:(h + 1) * 128],
                                   in_=kvf[:, h * 128:(h + 1) * 128]),
                            deps=[t_combh[h]], sig=True)

    t_p2 = {}
    prev_qdb_read = None
    prev_p2b_read = None
    for h in range(H):
        last = None
        for b in range(NBLK):
            if b == 0:
                rhs_tok, rhs_ap = t_qd[h], qd[:, h * TC:h * TC + 256]
            else:
                rhs_tok = kb.comp("vector", lambda e, h=h, b=b: e.tensor_scalar(
                    out=qdb, in0=qd[:, h * TC + b * 256:h * TC + b * 256 + 256],
                    scalar1=percol[:, h * 16 + b - 1:h * 16 + b],
                    scalar2=None, op0=ALU.mult),
                    deps=[t_qd[h], prev_qdb_read], sig=True)
                rhs_ap = qdb
            j, bank, bfree = fb.get()
            tmm = kb.comp("tensor", lambda e, h=h, rhs_ap=rhs_ap, bank=bank:
                          e.matmul(bank[:, 0:256], kvbf[:, h * 128:(h + 1) * 128],
                                   rhs_ap, start=True, stop=True),
                          deps=[t_kvin[h], rhs_tok, bfree], sig=True)
            if b > 0:
                prev_qdb_read = tmm
            tcp = kb.comp("scalar", lambda e, bank=bank: e.copy(
                out=p2b, in_=bank[:, 0:256]),
                deps=[tmm, prev_p2b_read], sig=True)
            fb.release(j, tcp)
            last = kb.comp("vector", lambda e, h=h, b=b: e.tensor_add(
                out=attn[:, h * TC + b * 256:h * TC + b * 256 + 256],
                in0=attn[:, h * TC + b * 256:h * TC + b * 256 + 256],
                in1=p2b), deps=[tcp, t_attn_w[h][b]], sig=True)
            prev_p2b_read = last
        t_p2[h] = last
    t_pass2_pe = kb.fence("sP")

    dump("attn", attn, [128, H * TC])
    dump("kvin", kvf, [128, H * 128])

    stat_banks = [fb.get(), fb.get()]
    t_sq = {}
    t_stat_mm = [None, None]
    for kc in range(KCN):
        tsq = kb.comp("scalar", lambda e, kc=kc: e.activation(
            out=sqs, in_=attn[:, kc * TC:(kc + 1) * TC], func=AF.Square,
            bias=0.0, scale=1.0),
            deps=[t_p2[kc], t_sq.get(kc - 1)], sig=True)
        for th in range(2):
            j, bank, bfree = stat_banks[th]
            t_stat_mm[th] = kb.comp(
                "tensor", lambda e, kc=kc, th=th, bank=bank: e.matmul(
                    bank, ones_bf,
                    sqs[:, th * 512:(th + 1) * 512],
                    start=(kc == 0), stop=(kc == KCN - 1)),
                deps=[tsq, bfree if kc == 0 else None, t_ones],
                sig=True)
        t_sq[kc] = t_stat_mm[1]
    t_rstd = []
    for th in range(2):
        j, bank, bfree = stat_banks[th]
        tr = kb.comp("scalar", lambda e, th=th, bank=bank: e.activation(
            out=rstd_rowf[:, th * 512:(th + 1) * 512], in_=bank[0:1, :],
            func=AF.Sqrt, bias=eps6[0:1, :], scale=1.0 / D),
            deps=[t_stat_mm[th]], sig=True)
        fb.release(j, tr)
        t_rstd.append(tr)
    t_rec = kb.comp("vector", lambda e: e.reciprocal(
        out=rstd_rowf, in_=rstd_rowf), deps=list(t_rstd), sig=True)
    t_rcast = kb.comp("scalar", lambda e: e.copy(
        out=rstd_rowb, in_=rstd_rowf), deps=[t_rec], sig=True)
    t_rd = kb.dma("gpsimd", rstd_dram.rearrange("(o n) -> o n", o=1), rstd_rowb,
                  deps=[t_rcast])
    dump("rstdrow", rstd_rowf, [1, TC])
    t_brstd = kb.dma("gpsimd", B_rstd,
                     rstd_dram.rearrange("(o n) -> o n", o=1)
                     .partition_broadcast(128)[:, 0, :],
                     deps=[t_rd, t_p2[H - 1]])

    t_epi = {}
    prev_t1b = None
    for kc in range(KCN):
        ta = kb.comp("vector", lambda e, kc=kc: e.tensor_mul(
            out=t1b, in0=attn[:, kc * TC:(kc + 1) * TC], in1=B_rstd),
            deps=[t_brstd, t_p2[kc], prev_t1b, t_sq.get(KCN - 1)], sig=True)
        tb = kb.comp("vector", lambda e, kc=kc: e.scalar_tensor_tensor(
            out=epi[:, kc * TC:(kc + 1) * TC], in0=t1b,
            scalar=normw_col[:, kc:kc + 1], in1=gate[:, kc * TC:(kc + 1) * TC],
            op0=ALU.mult, op1=ALU.mult),
            deps=[ta, t_nw, t_gate_ev[(kc, 0)], t_gate_ev[(kc, 1)],
                  t_pass2_pe, t_kvin[H - 1]],
            sig=True)
        prev_t1b = tb
        t_epi[kc] = tb
    t_epi_done = kb.fence("sV")

    dump("epi", epi, [128, KCN * TC])

    t_x1ev = {}
    wob_free = None
    prev_tev = None
    for fh in range(2):
        t_conv = {}
        for kc in range(KCN):
            twl = kb.dma("gpsimd", woutb[:, kc * 1024:(kc + 1) * 1024],
                         wout[fh, kc],
                         deps=[t_epi_done if (fh == 0 and kc < 2) else None,
                               wob_free],
                         sem=f"wo{kc % 2}")
            t_conv[kc] = twl
        if fh == 0:
            t_bga = kb.dma("gpsimd", B_ga, bcast(rows_dram.rearrange("a b -> (a b)"),
                                                 0, D), deps=[t_garow, t_epi_done])
        mm_last = None
        for tc in range(NTC):
            for fg in range(2):
                j, bank, bfree = fb.get()
                for kc in range(KCN):
                    mm_last = kb.comp(
                        "tensor",
                        lambda e, fh=fh, tc=tc, fg=fg, kc=kc, bank=bank: e.matmul(
                            bank,
                            epi[:, kc * TC + tc * 128:kc * TC + (tc + 1) * 128],
                            woutb[:, kc * 1024 + fg * 512:
                                  kc * 1024 + fg * 512 + 512],
                            start=(kc == 0), stop=(kc == KCN - 1)),
                        deps=[t_conv[kc], t_epi[kc], bfree],
                        sig=(kc == KCN - 1))
                fcol = fh * 1024 + fg * 512
                tt = kb.comp("vector", lambda e, fcol=fcol, bank=bank:
                             e.tensor_mul(out=silus, in0=bank,
                                          in1=B_ga[:, fcol:fcol + 512]),
                             deps=[mm_last, t_bga, prev_tev], sig=True)
                tev = kb.comp("vector", lambda e, tc=tc, fcol=fcol:
                              e.tensor_add(
                                  out=x1[:, tc * D + fcol:tc * D + fcol + 512],
                                  in0=x1[:, tc * D + fcol:tc * D + fcol + 512],
                                  in1=silus),
                              deps=[tt, t_x1[-1]], sig=True)
                fb.release(j, tev)
                t_x1ev[(fh, tc, fg)] = tev
                prev_tev = tev
        wob_free = mm_last
    t_wout_done = kb.fence("sP")

    dump("x1a", x1, [128, NTC * D])

    t_stat2 = []
    t_ev_all = t_x1ev[(1, NTC - 1, 1)]
    for tc in range(NTC):
        deps = [t_x1ev[(fh, tc, fg)] for fh in range(2) for fg in range(2)]
        deps.append(t_ev_all)
        tid = kb.comp("scalar", lambda e, tc=tc: e.activation(
            out=x1[:, tc * D:(tc + 1) * D],
            in_=x1[:, tc * D:(tc + 1) * D], func=AF.Identity,
            bias=0.0, scale=1.0, accum_out=cols[:, tc:tc + 1]),
            deps=deps + [t_wout_done], sig=True)
        t_stat2.append(kb.comp("scalar", lambda e, tc=tc: e.activation(
            out=t1, in_=x1[:, tc * D:(tc + 1) * D], func=AF.Square,
            bias=0.0, scale=1.0, accum_out=cols[:, 8 + tc:9 + tc]),
            deps=deps + [tid], sig=True))
    t_m8b = kb.comp("vector", lambda e: e.tensor_scalar_mul(
        out=cols[:, 16:24], in0=cols[:, 0:8], scalar1=1.0 / D),
        deps=[t_stat2[-1]], sig=True)
    t_msqb = kb.comp("scalar", lambda e: e.activation(
        out=cols[:, 24:32], in_=cols[:, 16:24], func=AF.Square, bias=0.0,
        scale=1.0), deps=[t_m8b], sig=True)
    t_varb = kb.comp("vector", lambda e: e.scalar_tensor_tensor(
        out=cols[:, 32:40], in0=cols[:, 8:16], scalar=1.0 / D,
        in1=cols[:, 24:32], op0=ALU.mult, op1=ALU.subtract),
        deps=[t_msqb], sig=True)
    t_sdb = kb.comp("scalar", lambda e: e.activation(
        out=cols[:, 48:56], in_=cols[:, 32:40], func=AF.Sqrt, bias=epscol,
        scale=1.0), deps=[t_varb], sig=True)
    t_r8b = kb.comp("vector", lambda e: e.reciprocal(
        out=cols[:, 40:48], in_=cols[:, 48:56]), deps=[t_sdb], sig=True)

    t_b1cb = kb.dma("gpsimd", B_1cb, bcast(agm_flat, 4 * D, 5 * D),
                    deps=[t_agm, t_wout_done, t_epi_done])
    t_b1cb = kb.comp("vector", lambda e: e.tensor_scalar_add(
        out=B_1cb, in0=B_1cb, scalar1=1.0), deps=[t_b1cb], sig=True)
    t_bsb = kb.dma("gpsimd", B_sb, bcast(agm_flat, 3 * D, 4 * D), deps=[t_agm])
    t_binv = kb.dma("gpsimd", B_invgb,
                    bcast(rows_dram.rearrange("a b -> (a b)"), 2 * D, 3 * D),
                    deps=[t_invrow, t_wout_done])

    sc_banks = [fb.get(), fb.get()]
    prev_tr2 = None
    t_h2 = {}
    t_div = {}
    t_sc_mm = [None, None]
    for tc in range(NTC):
        th = tc // 4
        tts = kb.comp("vector", lambda e, tc=tc: e.tensor_scalar(
            out=t1, in0=x1[:, tc * D:(tc + 1) * D],
            scalar1=cols[:, 16 + tc:17 + tc], scalar2=cols[:, 40 + tc:41 + tc],
            op0=ALU.subtract, op1=ALU.mult),
            deps=[t_r8b, prev_tr2], sig=True)
        tt1 = kb.comp("vector", lambda e: e.tensor_mul(
            out=t1, in0=t1, in1=B_1cb), deps=[tts, t_b1cb], sig=True)
        tt2 = kb.comp("vector", lambda e: e.tensor_add(
            out=t1, in0=t1, in1=B_sb), deps=[tt1, t_bsb], sig=True)
        for kc in range(KCN):
            j, bank, bfree = fb.get()
            ttr = kb.comp("tensor", lambda e, kc=kc, bank=bank: e.transpose(
                bank[:, 0:128], t1[:, kc * 128:(kc + 1) * 128], ident32),
                deps=[tt2, bfree], sig=True)
            tev = kb.comp("scalar", lambda e, tc=tc, kc=kc, bank=bank: e.copy(
                out=h2T[:, kc * TC + tc * 128:kc * TC + (tc + 1) * 128],
                in_=bank[:, 0:128]), deps=[ttr], sig=True)
            fb.release(j, tev)
            prev_tr2 = ttr
            t_h2[(tc, kc)] = tev
        t_div[tc] = kb.comp("vector", lambda e, tc=tc: e.tensor_mul(
            out=x1[:, tc * D:(tc + 1) * D], in0=x1[:, tc * D:(tc + 1) * D],
            in1=B_invgb), deps=[tts, t_binv,
                                t_stat2[-1]], sig=True)

    t_gw = kb.dma("gpsimd", gatew_sb, gatew, sem="gw",
                  deps=[t_wout_done, t_epi_done])
    t_sc_ev = []
    for th in range(2):
        j, bank, bfree = sc_banks[th]
        for kc in range(KCN):
            t_sc_mm[th] = kb.comp(
                "tensor", lambda e, kc=kc, th=th, bank=bank: e.matmul(
                    bank,
                    gatew_sb[:, kc * 128:(kc + 1) * 128],
                    h2T[:, kc * TC + th * 512:kc * TC + th * 512 + 512],
                    start=(kc == 0), stop=(kc == KCN - 1)),
                deps=[t_gw, bfree if kc == 0 else None,
                      t_h2[(th * 4, kc)], t_h2[(th * 4 + 1, kc)],
                      t_h2[(th * 4 + 2, kc)], t_h2[(th * 4 + 3, kc)]],
                sig=(kc == KCN - 1))
        tev = kb.comp("scalar", lambda e, th=th, bank=bank: e.copy(
            out=scorest[:, th * 512:(th + 1) * 512], in_=bank[0:8, 0:512]),
            deps=[t_sc_mm[th]], sig=True)
        fb.release(j, tev)
        t_sc_ev.append(tev)

    t_comb_tok = {}
    prev_soft = None
    for tc in range(NTC):
        th = tc // 4
        j, bank, bfree = fb.get()
        ttr = kb.comp("tensor", lambda e, tc=tc, bank=bank: e.transpose(
            bank[:, 0:8], scorest[0:8, tc * 128:(tc + 1) * 128],
            ident32[0:8, 0:8]), deps=[t_sc_ev[th], bfree], sig=True)
        tcp = kb.comp("vector", lambda e, bank=bank: e.tensor_copy(
            out=sctok, in_=bank[:, 0:8]), deps=[ttr, prev_soft], sig=True)
        fb.release(j, tcp)
        dump(f"sctok{tc}", sctok, [128, 8])
        tm1 = kb.comp("vector", lambda e: e.reduce_max(
            out=smt[:, 0:1], in_=sctok, axis=AXX), deps=[tcp], sig=True)
        tneg = kb.comp("vector", lambda e: e.tensor_scalar_mul(
            out=smt[:, 1:2], in0=smt[:, 0:1], scalar1=-1.0), deps=[tm1], sig=True)
        texp = kb.comp("scalar", lambda e: e.activation(
            out=smt[:, 8:16], in_=sctok, func=AF.Exp, bias=smt[:, 1:2],
            scale=1.0, accum_out=smt[:, 2:3]), deps=[tneg], sig=True)
        trec = kb.comp("vector", lambda e: e.reciprocal(
            out=smt[:, 3:4], in_=smt[:, 2:3]), deps=[texp], sig=True)
        tsm = kb.comp("vector", lambda e: e.tensor_scalar(
            out=smt[:, 8:16], in0=smt[:, 8:16], scalar1=smt[:, 3:4],
            scalar2=None, op0=ALU.mult), deps=[trec], sig=True)
        tmx = kb.comp("vector", lambda e: e.reduce_max(
            out=smt[:, 4:5], in_=smt[:, 8:16], axis=AXX), deps=[tsm], sig=True)
        tm1m = kb.comp("vector", lambda e: e.tensor_scalar(
            out=smt[:, 16:24], in0=smt[:, 8:16], scalar1=smt[:, 4:5],
            scalar2=None, op0=ALU.is_ge), deps=[tmx], sig=True)
        ts2 = kb.comp("vector", lambda e: e.scalar_tensor_tensor(
            out=smt[:, 24:32], in0=smt[:, 16:24], scalar=-1e30,
            in1=smt[:, 8:16], op0=ALU.mult, op1=ALU.add), deps=[tm1m], sig=True)
        tmx2 = kb.comp("vector", lambda e: e.reduce_max(
            out=smt[:, 5:6], in_=smt[:, 24:32], axis=AXX), deps=[ts2], sig=True)
        tm2m = kb.comp("vector", lambda e: e.tensor_scalar(
            out=smt[:, 24:32], in0=smt[:, 24:32], scalar1=smt[:, 5:6],
            scalar2=None, op0=ALU.is_ge), deps=[tmx2], sig=True)
        tmsk = kb.comp("vector", lambda e: e.tensor_add(
            out=smt[:, 16:24], in0=smt[:, 16:24], in1=smt[:, 24:32]),
            deps=[tm2m], sig=True)
        prev_soft = kb.comp("vector", lambda e, tc=tc: e.tensor_mul(
            out=comb[:, tc * 8:(tc + 1) * 8], in0=smt[:, 8:16],
            in1=smt[:, 16:24]), deps=[tmsk], sig=True)
        t_comb_tok[tc] = prev_soft

    dump("h2T", h2T, [128, KCN * TC])
    dump("scores", scorest, [8, TC])
    dump("comb", comb, [128, 64])
    dump("x1b", x1, [128, NTC * D])

    w13_free = [None, None]
    w2_free = [None, None]
    t_gff_last = None
    last_expert_mm = kb.fence("sP")
    x1_dep = {tc: [t_div[tc]] for tc in range(NTC)}
    scorest_dead = [t_sc_ev[0], t_sc_ev[1], t_comb_tok[NTC - 1]]

    for ex in range(E + 1):
        shared = (ex == 0)
        w13src = ws13 if shared else w13[ex - 1]
        w2src = ws2d if shared else w2d[ex - 1]
        t_g = {}
        for ic in range(ICN):
            mm_tok = {}
            bank_of = {}
            for m in range(2):
                slot = m
                twl = kb.dma("gpsimd", w13b[slot], w13src[ic, m],
                             deps=[w13_free[slot]] +
                             ([t_wout_done, t_div[NTC - 1]]
                              if (ex == 0 and ic == 0) else []),
                             sem=f"f{slot}")
                for th in range(2):
                    j, bank, bfree = fb.get()
                    for kc in range(KCN):
                        tok = kb.comp(
                            "tensor",
                            lambda e, slot=slot, kc=kc, th=th, bank=bank:
                            e.matmul(
                                bank,
                                w13b[slot][:, kc * 128:(kc + 1) * 128],
                                h2T[:, kc * TC + th * 512:kc * TC + th * 512 + 512],
                                start=(kc == 0), stop=(kc == KCN - 1)),
                            deps=[twl, bfree,
                                  t_h2[(th * 4 + 3, kc)], t_h2[(th * 4, kc)],
                                  t_h2[(th * 4 + 1, kc)], t_h2[(th * 4 + 2, kc)]],
                            sig=(kc == KCN - 1))
                    mm_tok[(m, th)] = tok
                    bank_of[(m, th)] = (j, bank)
                w13_free[slot] = mm_tok[(m, 1)]
            for th in range(2):
                j1, bankA = bank_of[(0, th)]
                j3, bankB = bank_of[(1, th)]
                tsl = kb.comp("scalar", lambda e, bankA=bankA: e.activation(
                    out=silus, in_=bankA, func=AF.Silu, bias=0.0, scale=1.0),
                    deps=[mm_tok[(0, th)], t_gff_last], sig=True)
                t_gff_last = kb.comp("vector",
                                     lambda e, ic=ic, th=th, bankB=bankB:
                                     e.tensor_mul(
                                         out=gff[:, ic * TC + th * 512:
                                                 ic * TC + th * 512 + 512],
                                         in0=silus, in1=bankB),
                                     deps=[tsl, mm_tok[(1, th)],
                                           last_expert_mm], sig=True)
                fb.release(j1, tsl)
                fb.release(j3, t_gff_last)
                t_g[(ic, th)] = t_gff_last
        for fg in range(4):
            extra = ([t_wout_done] if (ex == 0 and fg == 0) else [])
            tw2a = kb.dma("gpsimd", w2b[0].rearrange("p (i f) -> p i f", i=4),
                          w2src[fg, 0:4].transpose([1, 0, 2]),
                          deps=[w2_free[0]] + extra, sem="u0")
            tw2b = kb.dma("gpsimd", w2b[1].rearrange("p (i f) -> p i f", i=4),
                          w2src[fg, 4:8].transpose([1, 0, 2]),
                          deps=[w2_free[1]] + extra, sem="u1")
            for tc in range(NTC):
                j, bank, bfree = fb.get()
                last = None
                for ic in range(ICN):
                    half, i4 = divmod(ic, 4)
                    tw = tw2a if half == 0 else tw2b
                    last = kb.comp(
                        "tensor",
                        lambda e, ic=ic, tc=tc, half=half, i4=i4, bank=bank:
                        e.matmul(
                            bank,
                            gff[:, ic * TC + tc * 128:ic * TC + (tc + 1) * 128],
                            w2b[half][:, i4 * 512:(i4 + 1) * 512],
                            start=(ic == 0), stop=(ic == ICN - 1)),
                        deps=[tw, bfree, t_g[(ic, tc // 4)]],
                        sig=(ic == 3 or ic == ICN - 1))
                    if ic == 3:
                        w2_free[0] = last
                    if ic == ICN - 1:
                        w2_free[1] = last
                if shared:
                    tev = kb.comp("vector", lambda e, tc=tc, fg=fg, bank=bank:
                                  e.tensor_add(
                                      out=x1[:, tc * D + fg * 512:tc * D + fg * 512 + 512],
                                      in0=x1[:, tc * D + fg * 512:tc * D + fg * 512 + 512],
                                      in1=bank),
                                  deps=[last] + x1_dep[tc], sig=True)
                else:
                    tev = kb.comp("vector",
                                  lambda e, tc=tc, fg=fg, ex=ex, bank=bank:
                                  e.scalar_tensor_tensor(
                                      out=x1[:, tc * D + fg * 512:tc * D + fg * 512 + 512],
                                      in0=bank,
                                      scalar=comb[:, tc * 8 + ex - 1:tc * 8 + ex],
                                      in1=x1[:, tc * D + fg * 512:tc * D + fg * 512 + 512],
                                      op0=ALU.mult, op1=ALU.add),
                                  deps=[last, t_comb_tok[tc]] + x1_dep[tc],
                                  sig=True)
                fb.release(j, tev)
                x1_dep[tc] = [tev]
        last_expert_mm = kb.fence("sP")

    dump("x1c", x1, [128, NTC * D])

    t_bgb = kb.dma("gpsimd", B_gb,
                   bcast(rows_dram.rearrange("a b -> (a b)"), 1 * D, 2 * D),
                   deps=[t_gbrow, last_expert_mm, t_gff_last])
    final = []
    for tc in range(NTC):
        tfin = kb.comp("vector", lambda e, tc=tc: e.tensor_mul(
            out=x1[:, tc * D:(tc + 1) * D], in0=x1[:, tc * D:(tc + 1) * D],
            in1=B_gb), deps=[t_bgb] + x1_dep[tc], sig=True)
        final.append(kb.dma("scalar", y_out[tc * 128:(tc + 1) * 128, :],
                            x1[:, tc * D:(tc + 1) * D], deps=[tfin]))
    kb.op("scalar", lambda e: e.nop(), deps=[final[-1]])
    kb.op("gpsimd", lambda e: e.nop(), deps=[kb.fence("sG")])

    with ExitStack() as stack:
        sems = {n: stack.enter_context(nc.semaphore(n)) for n in kb.cnt}
        with nc.Block() as block:
            def mk(name):
                def body(eng):
                    for waits, fn, sig, inc in kb.q[name]:
                        for (s, v) in waits:
                            eng.wait_ge(sems[s], v)
                        ins = fn(eng)
                        if sig is not None:
                            ins.then_inc(sems[sig], inc)
                return body
            block.sync(mk("sync"))
            block.tensor(mk("tensor"))
            block.vector(mk("vector"))
            block.scalar(mk("scalar"))
            block.gpsimd(mk("gpsimd"))
    return nc


_NC_CACHE = {}


def kernel(L0_attn_hidden_states, L0_attn_slope_rate, L0_attn_w_qkv,
           L0_attn_w_gate, L0_attn_w_out, L0_attn_norm_w, L0_ffn_gate_w,
           L0_ffn_w1, L0_ffn_w2, L0_ffn_w3, L0_ffn_ws1, L0_ffn_ws2,
           L0_ffn_ws3, L0_c, L0_wmod):
    w = _prep_weights(L0_attn_w_qkv, L0_attn_w_gate, L0_attn_w_out, L0_wmod,
                      L0_ffn_gate_w, L0_ffn_w1, L0_ffn_w2, L0_ffn_w3,
                      L0_ffn_ws1, L0_ffn_ws2, L0_ffn_ws3)
    x = np.ascontiguousarray(np.asarray(L0_attn_hidden_states,
                                        dtype=np.float32).reshape(T, D))
    slope_a = np.ascontiguousarray(
        np.asarray(L0_attn_slope_rate, dtype=np.float32).reshape(H, 1))
    cvec_a = np.ascontiguousarray(np.asarray(L0_c, dtype=np.float32).reshape(D))
    normw_a = np.ascontiguousarray(
        np.asarray(L0_attn_norm_w, dtype=np.float32).reshape(D))
    mask = _mask01()
    ident = np.eye(128, dtype=np.float32)

    if "nc" not in _NC_CACHE:
        _NC_CACHE["nc"] = build_nc()
    nc = _NC_CACHE["nc"]

    in_maps = []
    for c in range(NC):
        in_maps.append({
            "x_in": np.ascontiguousarray(x[c * TC:(c + 1) * TC]),
            "wqkv": w["wqkv"], "wgate": w["wgate"], "wout": w["wout"],
            "wmod": w["wmod_percore"][c], "gatew": w["gatew"],
            "w13": w["w13"], "ws13": w["ws13"], "w2d": w["w2"], "ws2d": w["ws2"],
            "slope": slope_a, "cvec": cvec_a, "normw": normw_a,
            "dkconst": _dk_const(c), "mask01d": mask, "identd": ident,
        })
    res = run_bass_kernel_spmd(nc, in_maps, core_ids=list(range(NC)))
    out = np.concatenate([res.results[c]["y_out"] for c in range(NC)], axis=0)
    return out.reshape(1, T, D)
